# version 16
# baseline (speedup 1.0000x reference)
import os
import numpy as np
from contextlib import ExitStack

import concourse.bass as bass
import concourse.mybir as mybir
from concourse.bass_utils import run_bass_kernel_spmd
from concourse.alu_op_type import AluOpType as Op

F32 = mybir.dt.float32
BF16 = mybir.dt.bfloat16
I32 = mybir.dt.int32
FP8 = mybir.dt.float8e4
AF = mybir.ActivationFunctionType
AX = mybir.AxisListType

S, D = 2048, 2048
NT = 16
NH, DH = 32, 64
FF = 8192
NBI, NBO = 12, 8
THETA = 10000.0

_2PI = 2.0 * np.pi
C1 = 6.28125
C2 = float((np.float32(_2PI - C1).view(np.uint32) & np.uint32(0xFFFFF000)).view(np.float32))
C3 = float(np.float32(_2PI - C1 - C2))
INV2PI = float(np.float32(1.0 / _2PI))
HPI = float(np.float32(np.pi / 2))
PCLIP = float(np.nextafter(np.float32(np.pi), np.float32(0)))
TWO_PI32 = float(np.float32(2.0 * np.pi))
NEG_PI32 = float(-np.float32(np.pi))
LNT32 = float(np.float32(-np.log(THETA) / 32.0))


class Sem:
    __slots__ = ("h", "n", "name", "chain")

    def __init__(self, ctx, nc, name):
        self.h = ctx.enter_context(nc.semaphore(name))
        self.n = 0
        self.name = name


class Build:
    ENGS = ("sp", "gp", "pe", "dve", "act")

    def __init__(self, nc, ctx):
        self.nc = nc
        self.qs = {e: [] for e in self.ENGS}
        self.es = {e: Sem(ctx, nc, "es_" + e) for e in ("pe", "dve", "act", "gp")}
        self.wm = {e: {} for e in self.ENGS}
        self.ctx = ctx

    def sem(self, name):
        return Sem(self.ctx, self.nc, name)

    def _push(self, eng, deps, emit, incs):
        dd = {}
        wm = self.wm[eng]
        own = self.es.get(eng)
        for t in deps:
            if t is None:
                continue
            s, v = t
            if wm.get(s, -1) >= v:
                continue
            if s not in dd or dd[s] < v:
                dd[s] = v
        for s, v in dd.items():
            wm[s] = v
        deps = [(s, v) for s, v in dd.items()]

        def run(h, deps=deps, emit=emit, incs=incs):
            for s, v in deps[1:]:
                h.wait_ge(s.h, v)
            ins = emit(h)
            if deps:
                ins._wait_ge(deps[0][0].h, deps[0][1])
            for s, amt in incs:
                ins.then_inc(s.h, amt)

        self.qs[eng].append(run)

    def op(self, eng, emit, *deps):
        s = self.es[eng]
        s.n += 1
        self._push(eng, deps, emit, [(s, 1)])
        return (s, s.n)

    def dma(self, eng, dsem, emit, *deps):
        prev = (dsem, dsem.n) if getattr(dsem, "chain", False) and dsem.n else None
        deps = list(deps) + [prev]
        if getattr(dsem, "chain", False):
            deps += [(s, s.n) for s in self.es.values() if s.n]
        dsem.n += 16
        self._push(eng, deps, emit, [(dsem, 16)])
        return (dsem, dsem.n)

    def raw(self, eng, fn):
        self.qs[eng].append(fn)

    def flush(self, block):
        mp = dict(pe=block.tensor, dve=block.vector, act=block.scalar,
                  gp=block.gpsimd, sp=block.sync)
        for e in self.ENGS:
            if self.qs[e]:
                fns = self.qs[e]
                self.qs[e] = []
                mp[e](lambda h, fns=fns: [f(h) for f in fns])


def build_program(debug=False, upto="full"):
    nc = bass.Bass("TRN2", target_bir_lowering=False)
    x_h = nc.dram_tensor("x", [S, D], F32, kind="ExternalInput").ap()
    wi_h = nc.dram_tensor("w_in_blk", [NBI, D, 256], F32, kind="ExternalInput").ap()
    wo_h = nc.dram_tensor("w_out_blk", [NBO, D, 256], F32, kind="ExternalInput").ap()
    rw_h = nc.dram_tensor("rms_w", [1, D], F32, kind="ExternalInput").ap()
    wgu_h = nc.dram_tensor("w_gu_pairs", [32, D, 512], F32, kind="ExternalInput").ap()
    wdo_h = nc.dram_tensor("w_do_T", [FF, D], F32, kind="ExternalInput").ap()
    out_h = nc.dram_tensor("out", [S, D], F32, kind="ExternalOutput").ap()
    sgu_h = nc.dram_tensor("scr_gu", [32, 128, 16 * 512], FP8, kind="Internal").ap()
    sdo_h = nc.dram_tensor("scr_do", [64, 128, D], FP8, kind="Internal").ap()

    dbg = {}
    if debug:
        for nm, shp, dt in [
            ("d_rms", [128, D], BF16), ("d_q", [128, D], BF16),
            ("d_kt", [128, S], BF16), ("d_v", [128, 512], BF16),
            ("d_ss", [128, 512], F32), ("d_probs", [128, 512], BF16),
            ("d_o", [128, D], BF16), ("d_x1", [128, D], BF16),
            ("d_sc", [1, 6], F32), ("d_tabs", [128, 2048], BF16), ("d_ang", [128, 1024], F32),
            ("d_g", [128, FF], BF16), ("d_gq", [128, FF], BF16),
            ("d_y2", [128, D], F32), ("d_aq", [128, D], BF16),
        ]:
            dbg[nm] = nc.dram_tensor(nm, shp, dt, kind="ExternalOutput").ap()

    ctx = ExitStack()
    with ctx:
        sb = lambda name, shape, dt=F32: ctx.enter_context(nc.sbuf_tensor(name, shape, dt))
        psum = lambda name, shape, dt=F32: ctx.enter_context(nc.psum_tensor(name, shape, dt))
        B = Build(nc, ctx)

        ident_b = sb("ident_b", [128, 128], BF16)
        mask = sb("mask", [128, 128], F32)
        ones_f = sb("ones_f", [128, 128], F32)
        onecol = sb("onecol", [128, 1], F32)
        svec = sb("svec", [128, 32], F32)
        r1_all = sb("r1_all", [128, NT], F32)
        bnb = sb("bnb", [128, 24], F32)
        bn2 = sb("bn2", [128, 2], F32)
        sca = sb("sca", [128, 8], F32)
        acc_gu = sb("acc_gu", [128, 256], F32)
        acc_do = sb("acc_do", [128, 128], F32)
        acc1 = sb("acc1", [1, 8], F32)
        tabs = sb("tabs", [128, 2, NT * 64], BF16)
        rms_w_bc = sb("rms_w_bc", [128, D], F32)
        xs = sb("xs", [128, 2, D], F32)
        stage = sb("stage", [128, 2, 1024], F32)
        st8 = sb("st8", [128, 2, 1024], FP8)
        sgu_rd = sb("sgu_rd", [128, 2, 2048], FP8)
        q32 = sb("q32", [128, 512], I32)
        qx1 = sb("qx1", [128, NT, D], BF16)
        arb = sb("arb", [128, 32768], BF16)
        ssat = sb("ssat", [128, D], F32)
        y2_sc = sb("y2_sc", [128, 2, D], F32)
        sil_sc = stage[:, 0, 0:512]
        w8s = sb("w8s", [128, 2, 4096], FP8)
        opair = sb("opair", [128, 128], BF16)

        kt_v = arb[:, 0:8192].rearrange("p (g s) -> p g s", g=4)
        v_v = arb[:, 8192:16384].rearrange("p (t c) -> p t c", t=16)
        wblk = arb[:, 16384:24576].rearrange("p (s k c) -> p s k c", s=2, k=16)
        rmsT = arb[:, 24576:32768].rearrange("p (k c) -> p k c", k=16)
        qt_v = arb[:, 24576:26624].rearrange("p (k c) -> p k c", k=16)
        pb_off = 26624
        ot_v = arb[:, 28672:32768].rearrange("p (t k c) -> p t k c", t=2, k=16)
        g_v = arb[:, 0:16384].rearrange("p (t f) -> p t f", t=2)
        gqt_v8 = arb[:, 24576:25600].rearrange("p (k c) -> p k c", k=8)
        aq_bufs = [arb[:, 25600:27648], arb[:, 27648:29696]]
        aqT_v = arb[:, 16384:20480].rearrange("p (t k c) -> p t k c", t=2, k=16)
        aq_st = arb[:, 20480:22528]
        gq_v = arb[:, 24576:32768]

        w8slots = [w8s[:, 0, :], w8s[:, 1, :]]
        pk = [psum("pk%d" % i, [128, 512], F32) for i in range(4)]
        po = psum("po", [128, 512], F32)
        tpb = [psum("tp%d" % i, [128, 1024], BF16) for i in range(3)]

        d_x = [B.sem("d_x0"), B.sem("d_x1")]
        d_wb = [B.sem("d_wb0"), B.sem("d_wb1")]
        d_st = [B.sem("d_st0"), B.sem("d_st1")]
        d_t8 = [B.sem("d_t80"), B.sem("d_t81")]
        d_sg = [B.sem("d_sg0"), B.sem("d_sg1")]
        d_sd = [B.sem("d_sd0"), B.sem("d_sd1")]
        d_w3 = [B.sem("d_w30"), B.sem("d_w31"), B.sem("d_w32")]
        d_out = B.sem("d_out")
        d_dbg = B.sem("d_dbg")
        d_dbg.chain = True
        d_misc = B.sem("d_misc")

        block = ctx.enter_context(nc.Block())

        R = {}

        def rr(k):
            return R.get(k, [])

        def rset(k, *toks):
            R[k] = [t for t in toks if t is not None]

        def radd(k, *toks):
            R.setdefault(k, []).extend(t for t in toks if t is not None)

        tp_state = {"bank": 0, "items": [], "last_tt": None,
                    "free": [[], [], []]}

        def tp_flush():
            st = tp_state
            if not st["items"]:
                return
            evs = []
            for (hld, pv, dst_ap, base, nrow, eng, dst_war) in st["items"]:
                if eng == "act":
                    tev = B.op("act", lambda h, pv=pv, dst_ap=dst_ap, base=base,
                               nrow=nrow: h.activation(
                        dst_ap, pv[base:base + nrow, :], AF.Copy),
                        st["last_tt"], *dst_war)
                else:
                    tev = B.op("dve", lambda h, pv=pv, dst_ap=dst_ap, base=base,
                               nrow=nrow: h.tensor_copy(
                        dst_ap, pv[base:base + nrow, :]),
                        st["last_tt"], *dst_war)
                hld[1] = tev
                evs.append(tev)
            st["free"][st["bank"]] = evs
            st["bank"] = (st["bank"] + 1) % 3
            st["items"] = []
            st["last_tt"] = None

        def tp_put(src_ap, dst_ap, base, nrow, eng, pre_deps, dst_war):
            st = tp_state
            if len(st["items"]) >= int(os.environ.get("TPB", "4")):
                tp_flush()
            qq = len(st["items"])
            pv = tpb[st["bank"]][:, qq * 128:(qq + 1) * 128]
            tt = B.op("pe", lambda h, pv=pv, src_ap=src_ap, base=base, nrow=nrow:
                      h.transpose(pv[base:base + nrow, :], src_ap, ident_b[:]),
                      *(list(pre_deps) + (st["free"][st["bank"]] if qq == 0 else [])
                        + [t_idb]))
            st["last_tt"] = tt
            hld = [tt, None]
            st["items"].append((hld, pv, dst_ap, base, nrow, eng, list(dst_war)))
            return hld

        def transpose_ev(src_ap, dst_ap, base, nrow, eng, pre_deps, dst_war):
            hld = tp_put(src_ap, dst_ap, base, nrow, eng, pre_deps, dst_war)
            tp_flush()
            return hld[0], hld[1]

        t_rw = B.dma("sp", d_misc, lambda h: h.dma_start(rms_w_bc[0:1, :], rw_h[:, :]))
        t_on = B.op("gp", lambda h: h.memset(ones_f[:], 1.0))
        t_oc = B.op("gp", lambda h: h.memset(onecol[:], 1.0))
        t_id = B.op("gp", lambda h: h.affine_select(
            mask[:], ones_f[:], pattern=[[-1, 128]], base=0, channel_multiplier=1,
            compare_op=Op.is_equal, fill=0.0), t_on)
        t_idb = B.op("dve", lambda h: h.tensor_copy(ident_b[:], mask[:]), t_id)
        t_m0 = B.op("gp", lambda h: h.memset(mask[:], 0.0), t_idb)
        t_mask = B.op("gp", lambda h: h.affine_select(
            mask[:], mask[:], pattern=[[-1, 128]], base=0, channel_multiplier=1,
            compare_op=Op.is_ge, fill=-1e30), t_m0)

        for c in range(4):
            tmm = B.op("pe", lambda h, c=c: h.matmul(
                pk[c][:, 0:512], lhsT=ones_f[0:1, 0:128],
                rhs=rms_w_bc[0:1, c * 512:(c + 1) * 512], start=True, stop=True),
                t_rw, t_on)
            tev = B.op("dve", lambda h, c=c: h.tensor_copy(
                rms_w_bc[:, c * 512:(c + 1) * 512], pk[c][:, 0:512]), tmm)
            rset(("pk", c), tev)
            radd("rwbc", tev)
        t_rb = rr("rwbc")[-1]

        angv = y2_sc[:, 0, 0:1024]
        a2v = y2_sc[:, 1, 0:1024]
        tgv = ssat[:, 0:1024]
        rrv = ssat[:, 1024:2048]
        t_tg = B.op("gp", lambda h: h.iota(tgv, pattern=[[128, 16], [0, 64]],
                    base=0, channel_multiplier=1, allow_small_or_imprecise_dtypes=True))
        t_eg = B.op("gp", lambda h: h.iota(angv, pattern=[[0, 16], [0, 2], [1, 32]],
                    base=0, channel_multiplier=0, allow_small_or_imprecise_dtypes=True))
        t_if = B.op("act", lambda h: h.activation(a2v, angv, AF.Exp, scale=LNT32), t_eg)
        t_ang = B.op("dve", lambda h: h.tensor_tensor(angv, tgv, a2v, Op.mult), t_if, t_tg)
        t_sin = t_cos = None
        for hv in range(2):
            sl5 = slice(hv * 512, (hv + 1) * 512)
            t_q = B.op("act", lambda h, sl5=sl5: h.activation(
                q32[:, :], angv[:, sl5], AF.Copy, scale=INV2PI, bias=0.5),
                t_ang, *rr("q32p"))
            t_r = t_q
            for i, cc in enumerate((C1, C2, C3)):
                t_r = B.op("dve", lambda h, cc=cc, i=i, sl5=sl5: h.scalar_tensor_tensor(
                    rrv[:, sl5], q32[:, :], -cc,
                    angv[:, sl5] if i == 0 else rrv[:, sl5], Op.mult, Op.add),
                    t_r, t_ang)
            t_c = B.op("dve", lambda h, sl5=sl5: h.tensor_scalar(
                tgv[:, sl5], rrv[:, sl5], NEG_PI32, None, Op.is_lt), t_r, t_ang)
            t_r = B.op("dve", lambda h, sl5=sl5: h.scalar_tensor_tensor(
                rrv[:, sl5], tgv[:, sl5], TWO_PI32, rrv[:, sl5], Op.mult, Op.add), t_c)
            t_r = B.op("gp", lambda h, sl5=sl5: h.tensor_scalar(
                rrv[:, sl5], rrv[:, sl5], PCLIP, -PCLIP, Op.min, Op.max), t_r)
            t_sin = B.op("act", lambda h, sl5=sl5: h.activation(
                tabs[:, 0, sl5], rrv[:, sl5], AF.Sin), t_r)
            t_q2 = B.op("act", lambda h, sl5=sl5: h.activation(
                q32[:, :], angv[:, sl5], AF.Copy, scale=INV2PI, bias=0.75),
                t_sin, t_q)
            rset("q32p", t_q2)
            t_a2 = B.op("dve", lambda h, sl5=sl5: h.tensor_scalar_add(
                a2v[:, sl5], angv[:, sl5], HPI), t_r, t_if)
            t_r2 = t_q2
            for i, cc in enumerate((C1, C2, C3)):
                t_r2 = B.op("dve", lambda h, cc=cc, i=i, sl5=sl5: h.scalar_tensor_tensor(
                    rrv[:, sl5], q32[:, :], -cc,
                    a2v[:, sl5] if i == 0 else rrv[:, sl5], Op.mult, Op.add),
                    t_r2, t_a2, t_sin)
            rset("q32p", t_r2)
            t_c2 = B.op("dve", lambda h, sl5=sl5: h.tensor_scalar(
                tgv[:, sl5], rrv[:, sl5], NEG_PI32, None, Op.is_lt), t_r2)
            t_r2 = B.op("dve", lambda h, sl5=sl5: h.scalar_tensor_tensor(
                rrv[:, sl5], tgv[:, sl5], TWO_PI32, rrv[:, sl5], Op.mult, Op.add), t_c2)
            t_r2 = B.op("gp", lambda h, sl5=sl5: h.tensor_scalar(
                rrv[:, sl5], rrv[:, sl5], PCLIP, -PCLIP, Op.min, Op.max), t_r2)
            t_cos = B.op("act", lambda h, sl5=sl5: h.activation(
                tabs[:, 1, sl5], rrv[:, sl5], AF.Sin), t_r2)
            rset("q32p", t_cos)
        t_r2 = t_cos
        rset("y2a", t_r2)
        rset("y2b", t_r2)
        rset(("ss", 0), t_cos)
        rset(("ss", 1), t_r2)
        rset("q32", t_r2)
        rset("tabs0", t_sin)
        rset("tabs1", t_cos)
        if debug:
            t_dt = B.dma("act", d_dbg, lambda h: h.dma_start(
                dbg["d_tabs"], tabs[:, :, :].rearrange("p a b -> p (a b)")), t_sin, t_cos)
            radd("tabs0", t_dt); radd("tabs1", t_dt)
            t_da = B.dma("act", d_dbg, lambda h: h.dma_start(dbg["d_ang"], angv), t_ang, t_q2)
            radd("y2a", t_da)

        PH = ["p0", "mean", "tern", "p1", "qkv", "attn", "ffn", "full"]
        LVL = PH.index(upto) if upto in PH else 6
        gu_flat = wgu_h.rearrange("b d c -> (b d c)")
        do_flat = wdo_h.rearrange("f d -> (f d)")
        U = 128 * 1024

        def mean_unit(i, flat, off, acc, col):
            sl = i % 2
            dst = stage[:, sl, :]
            td = B.dma("sp", d_st[sl], lambda h, dst=dst, flat=flat, off=off:
                       h.dma_start(dst, flat[off:off + U].rearrange("(p c) -> p c", p=128)),
                       *rr(("stg", sl)))
            tr = B.op("dve", lambda h, acc=acc, col=col, dst=dst: h.tensor_reduce(
                acc[:, col:col + 1], dst, axis=AX.X, op=Op.add,
                apply_absolute_value=True), td)
            rset(("stg", sl), tr)
            return tr

        mean_jobs = []
        mean_done = [None, None]
        ui = 0
        for i in range(256 if LVL >= 1 else 0):
            mean_jobs.append((ui, gu_flat, i * U, acc_gu, i, 0)); ui += 1
        for i in range(128 if LVL >= 1 else 0):
            mean_jobs.append((ui, do_flat, i * U, acc_do, i, 1)); ui += 1

        def pump_mean(k):
            while k > 0 and mean_jobs:
                a = mean_jobs.pop(0)
                mean_done[a[5]] = mean_unit(*a[:5])
                k -= 1

        sca_toks = []
        sca_hold = [None]

        def fin_mean():
            nonlocal sca_toks
            if LVL < 1 or sca_hold[0] is not None:
                return
            pump_mean(10 ** 9)
            for (acc, ncol, scale, hcol, scol, dep) in [
                (acc_gu, 256, 2.0 ** -25, 0, 4, mean_done[0]),
                (acc_do, 128, 2.0 ** -24, 2, 5, mean_done[1]),
            ]:
                t1 = B.op("dve", lambda h, acc=acc, n=ncol: h.tensor_reduce(
                    svec[:, 0:1], acc[:, 0:n], axis=AX.X, op=Op.add), dep, *rr("sv0"))
                rset("sv0", t1)
                tm2 = B.op("pe", lambda h: h.matmul(pk[0][0:1, 0:1], lhsT=onecol[:],
                           rhs=svec[:, 0:1], start=True, stop=True), t1, t_oc,
                           *rr(("pk", 0)))
                radd("sv0", tm2)
                t2 = B.op("dve", lambda h: h.tensor_copy(acc1[0:1, 0:1], pk[0][0:1, 0:1]), tm2)
                rset(("pk", 0), t2)
                t3 = B.op("dve", lambda h, s=scale: h.tensor_scalar(
                    acc1[0:1, 1:2], acc1[0:1, 0:1], s, 1e-5, Op.mult, Op.max), t2)
                t4 = B.op("dve", lambda h: h.tensor_scalar(
                    acc1[0:1, 2:3], acc1[0:1, 1:2], 0.5, None, Op.mult), t3)
                tm3 = B.op("pe", lambda h: h.matmul(pk[0][:, 0:2], lhsT=ones_f[0:1, 0:128],
                           rhs=acc1[0:1, 1:3], start=True, stop=True), t4, *rr(("pk", 0)))
                t5 = B.op("dve", lambda h, sc=scol: h.tensor_copy(
                    sca[:, sc:sc + 1], pk[0][:, 0:1]), tm3)
                t6 = B.op("dve", lambda h, hc=hcol: h.tensor_copy(
                    sca[:, hc:hc + 1], pk[0][:, 1:2]), tm3)
                t7 = B.op("dve", lambda h, hc=hcol: h.tensor_scalar(
                    sca[:, hc + 1:hc + 2], sca[:, hc:hc + 1], -1.0, None, Op.mult), t6)
                rset(("pk", 0), t6)
                sca_toks += [t5, t7]
            sca_hold[0] = sca_toks[-1]
            if debug:
                B.dma("act", d_dbg, lambda h: h.dma_start(dbg["d_sc"], sca[0:1, 0:6]),
                      *sca_toks)

        def tern_unit(i, src_ap, dst_ap, hcol):
            sl = i % 2
            stg = stage[:, sl, :]
            scr = y2_sc[:, 0, sl * 1024:(sl + 1) * 1024]
            out8 = st8[:, sl, 0:1024]
            dst3 = stg.rearrange("p (k c) -> p k c", c=512) if src_ap.ndim == 3 else stg
            td = B.dma("sp", d_st[sl], lambda h, dst3=dst3, src_ap=src_ap:
                       h.dma_start(dst3, src_ap), *rr(("stg", sl)))
            tc = B.op("gp", lambda h, scr=scr, stg=stg, hcol=hcol: h.tensor_scalar(
                scr, stg, sca[:, hcol:hcol + 1], sca[:, hcol + 1:hcol + 2],
                Op.min, Op.max), td, sca_hold[0],
                *(rr(("tscr", sl)) + rr("y2a")))
            ts = B.op("dve", lambda h, scr=scr, stg=stg: h.tensor_tensor(
                scr, stg, scr, Op.subtract), tc)
            rset(("stg", sl), ts)
            tn = B.op("act", lambda h, out8=out8, scr=scr: h.activation(
                out8, scr, AF.Sign), ts, *rr(("t8", sl)))
            rset(("tscr", sl), tn)
            radd("y2a", tn)
            tw = B.dma("act", d_t8[sl], lambda h, dst_ap=dst_ap, out8=out8:
                       h.dma_start(dst_ap, out8), tn)
            rset(("t8", sl), tw)
            return tw

        tern_jobs = []
        tern_gu = [None] * 32
        tern_do = [None] * 64
        ui = 0
        for j in range(32 if LVL >= 2 else 0):
            for q in range(8):
                src = wgu_h[j, q * 256:(q + 1) * 256, :].rearrange(
                    "(k p) c -> p k c", p=128)
                tern_jobs.append((ui, src, sgu_h[j, :, q * 1024:(q + 1) * 1024], 0,
                                  ("gu", j)))
                ui += 1
        for kc in range(64 if LVL >= 2 else 0):
            for q in range(2):
                src = wdo_h[kc * 128:(kc + 1) * 128, q * 1024:(q + 1) * 1024]
                tern_jobs.append((ui, src, sdo_h[kc, :, q * 1024:(q + 1) * 1024], 2,
                                  ("do", kc)))
                ui += 1

        def pump_tern(k):
            if sca_hold[0] is None:
                return
            while k > 0 and tern_jobs:
                a = tern_jobs.pop(0)
                tw = tern_unit(*a[:4])
                kind, idx = a[4]
                if kind == "gu":
                    tern_gu[idx] = (tern_gu[idx] or []) + [tw]
                else:
                    tern_do[idx] = (tern_do[idx] or []) + [tw]
                k -= 1

        def stats2048(src_ap, *dep):
            ts = None
            for j in range(4):
                ts = B.op("dve", lambda h, j=j, src_ap=src_ap: h.bn_stats(
                    bnb[:, j * 6:(j + 1) * 6], src_ap[:, j * 512:(j + 1) * 512]),
                    *(list(dep) + [ts] + rr("bnb")))
            ta = B.op("dve", lambda h: h.bn_aggr(bn2[:], bnb[:, 0:24]), ts, *rr("bn2"))
            rset("bnb", ta)
            rset("bn2", ta)
            return ta

        def recip_sqrt(col, src_col, eps, *dep):
            t0 = B.op("dve", lambda h, src_col=src_col, eps=eps: h.tensor_scalar(
                svec[:, 30:31], src_col, eps, None, Op.add),
                *(list(dep) + rr("sv30") + rr("bn2")))
            radd("bn2", t0)
            ta = B.op("act", lambda h: h.activation(
                svec[:, 30:31], svec[:, 30:31], AF.Sqrt), t0)
            rset("sv30", ta)
            tb = B.op("dve", lambda h, col=col: h.reciprocal(
                svec[:, col:col + 1], svec[:, 30:31]), ta, *rr(("sv", col)))
            radd("sv30", tb)
            rset(("sv", col), tb)
            return tb

        wblk_tok = [[], []]
        xs_tok = [[], []]
        kt_tok = {}
        v_tok = [None] * NT
        v_rd = []
        q_tok = [None] * NT
        qrd_tok = [[] for _ in range(NT)]
        r1_tok = [None] * NT
        x1_tok = [None] * NT
        po_tok = None
        arb_hi_war = [t_cos]
        out_toks = []

        for half in range(2 if LVL >= 3 else 0):
            for qq in range(2):
                qtiles = list(range(half * 8 + qq * 4, half * 8 + qq * 4 + 4))
                if (half, qq) != (0, 0):
                    fin_mean()
                P1N = int(os.environ.get("P1N", "9"))
                P1T = int(os.environ.get("P1T", "99"))
                for ti, t in enumerate(qtiles):
                    if t >= P1T:
                        continue
                    if (half, qq) == (0, 0):
                        pump_mean(12)
                    sl = t % 2
                    tx = B.dma("sp", d_x[sl], lambda h, sl=sl, t=t: h.dma_start(
                        xs[:, sl, :], x_h[t * 128:(t + 1) * 128, :]), *xs_tok[sl])
                    xs_tok[sl] = []
                    if P1N < 2:
                        continue
                    tst = stats2048(xs[:, sl, :], tx)
                    if P1N < 3:
                        continue
                    tr1 = recip_sqrt(2, bn2[:, 1:2], 1e-5, tst)
                    tcp = B.op("dve", lambda h, t=t: h.tensor_copy(
                        r1_all[:, t:t + 1], svec[:, 2:3]), tr1, *rr(("r1w", t)))
                    rset(("r1w", t), tcp)
                    r1_tok[t] = tcp
                    td = B.op("dve", lambda h: h.tensor_tensor(
                        svec[:, 3:4], bn2[:, 1:2], svec[:, 2:3], Op.mult), tr1, tst,
                        *rr(("sv", 3)))
                    radd("bn2", td)
                    td = B.op("dve", lambda h: h.tensor_tensor(
                        svec[:, 3:4], svec[:, 3:4], svec[:, 2:3], Op.mult), td)
                    rset(("sv", 3), td)
                    trq = recip_sqrt(4, svec[:, 3:4], 1e-6, td)
                    radd(("sv", 3), trq)
                    t5 = B.op("dve", lambda h: h.tensor_tensor(
                        svec[:, 5:6], svec[:, 2:3], svec[:, 4:5], Op.mult), trq, tr1,
                        *rr(("sv", 5)))
                    radd(("sv", 2), t5, td, tcp)
                    radd(("sv", 4), t5)
                    rset(("sv", 5), t5)
                    if P1N < 4:
                        continue
                    ta = B.op("dve", lambda h, sl=sl: h.scalar_tensor_tensor(
                        y2_sc[:, 0, :], xs[:, sl, :], bn2[:, 0:1], rms_w_bc[:],
                        Op.subtract, Op.mult), tst, t_rb, tx,
                        *(rr("y2a") + rr("y2b")))
                    radd("bn2", ta)
                    xs_tok[sl].append(ta)
                    if P1N < 5:
                        continue
                    rmsbf = qx1[:, t, :]
                    tb = B.op("dve", lambda h, rb=rmsbf: h.tensor_scalar(
                        rb, y2_sc[:, 0, :], svec[:, 5:6], None, Op.mult), ta, t5,
                        q_tok[t], *qrd_tok[t])
                    radd(("sv", 5), tb)
                    rset("y2a", tb)
                    rset("y2b", tb)
                    if debug and t == 0:
                        tdmp = B.dma("act", d_dbg, lambda h, rb=rmsbf: h.dma_start(
                            dbg["d_rms"], rb), tb)
                        qrd_tok[t].append(tdmp)
                    if P1N < 6:
                        continue
                    hlds = []
                    for kc in range(16):
                        hlds.append(tp_put(
                            rmsbf[:, kc * 128:(kc + 1) * 128],
                            rmsT[:, kc, ti * 128:(ti + 1) * 128], 0, 128, "act",
                            [tb], rr(("rmsT", kc, ti)) +
                            (arb_hi_war if (ti == 0 and kc == 0) else [])))
                    tp_flush()
                    for kc in range(16):
                        rset(("rmsT", kc, ti), hlds[kc][1])
                    qrd_tok[t].append(hlds[-1][1])
                for nb in range(NBI if LVL >= 4 else 0):
                    if (half, qq) == (0, 0):
                        pump_mean(12)
                    gsl = nb % 2
                    twb = B.dma("gp", d_wb[gsl], lambda h, nb=nb, gsl=gsl: h.dma_start(
                        wblk[:, gsl], wi_h[nb].rearrange("(k p) c -> p k c", p=128)),
                        *(wblk_tok[gsl] + (rr(("pb", 1)) if gsl == 1 else [])))
                    wblk_tok[gsl] = []
                    for mt in range(4):
                        t = qtiles[mt]
                        bank = mt
                        tlast = None
                        for kc in range(16):
                            tlast = B.op("pe", lambda h, kc=kc, mt=mt, bank=bank, gsl=gsl:
                                h.matmul(pk[bank][:, 0:256],
                                         lhsT=rmsT[:, kc, mt * 128:(mt + 1) * 128],
                                         rhs=wblk[:, gsl, kc, :],
                                         start=(kc == 0), stop=(kc == 15)),
                                twb, *(rr(("rmsT", kc, mt)) +
                                       (rr(("pk", bank)) if kc == 0 else [])))
                            radd(("rmsT", kc, mt), tlast)
                        wblk_tok[gsl].append(tlast)
                        if gsl == 1:
                            radd(("pb", 1), tlast)
                        pb = pk[bank][:, 0:256]
                        co = tabs[:, 1, t * 64:(t + 1) * 64]
                        si = tabs[:, 0, t * 64:(t + 1) * 64]
                        if nb < 10:
                            scl = 0.125 if nb < 8 else 1.0
                            t1 = t2a = t2b = None
                            ya = rr("y2a")
                            yb = rr("y2b")
                            for u in range(4):
                                o0 = u * 64
                                t1 = B.op("dve", lambda h, pb=pb, co=co, scl=scl, o0=o0:
                                    h.scalar_tensor_tensor(
                                        y2_sc[:, 0, o0:o0 + 64],
                                        pb[:, o0:o0 + 64], scl, co[:, 0:64],
                                        Op.mult, Op.mult), tlast,
                                    *(ya if u == 0 else []))
                                t2a = B.op("dve", lambda h, pb=pb, si=si, scl=scl, o0=o0:
                                    h.scalar_tensor_tensor(
                                        y2_sc[:, 1, o0:o0 + 32],
                                        pb[:, o0 + 32:o0 + 64], -scl, si[:, 0:32],
                                        Op.mult, Op.mult), tlast,
                                    *((yb + rr(("ss", 1))) if u == 0 else []))
                                t2b = B.op("dve", lambda h, pb=pb, si=si, scl=scl, o0=o0:
                                    h.scalar_tensor_tensor(
                                        y2_sc[:, 1, o0 + 32:o0 + 64],
                                        pb[:, o0:o0 + 32], scl, si[:, 0:32],
                                        Op.mult, Op.mult), tlast)
                            radd("tabs1", t1)
                            radd("tabs0", t2a, t2b)
                            rset(("pk", bank), t2b)
                            radd(("pk", bank), t1, t2a)
                        if nb < 8:
                            qv = qx1[:, t, nb * 256:(nb + 1) * 256]
                            tq = B.op("dve", lambda h, qv=qv: h.tensor_tensor(
                                qv, y2_sc[:, 0, 0:256], y2_sc[:, 1, 0:256], Op.add),
                                t1, t2a, t2b, *qrd_tok[t])
                            rset("y2a", tq)
                            rset("y2b", tq)
                            q_tok[t] = tq
                            if debug and t == 0 and nb == 7:
                                tdq = B.dma("act", d_dbg, lambda h, t=t: h.dma_start(
                                    dbg["d_q"], qx1[:, t, :]), tq)
                                qrd_tok[t].append(tdq)
                        elif nb < 10:
                            tk_ev = None
                            for hv in range(2):
                                tqk = B.op("dve", lambda h, hv=hv: h.tensor_tensor(
                                    opair[:, :], y2_sc[:, 0, hv * 128:(hv + 1) * 128],
                                    y2_sc[:, 1, hv * 128:(hv + 1) * 128], Op.add),
                                    t1, t2a, t2b, *rr("opair"))
                                gpair = (nb - 8) * 2 + hv
                                tt, tev = transpose_ev(
                                    opair[:, :], kt_v[:, gpair, t * 128:(t + 1) * 128],
                                    0, 128, "act", [tqk], rr(("kt", gpair, t)))
                                rset("opair", tt)
                                rset(("kt", gpair, t), tev)
                                kt_tok[(gpair, t)] = tev
                                tk_ev = tev
                            rset("y2a", tk_ev)
                            rset("y2b", tk_ev)
                        else:
                            voff = (nb - 10) * 256
                            tv = B.op("act", lambda h, bank=bank, t=t, voff=voff:
                                h.activation(v_v[:, t, voff:voff + 256],
                                             pk[bank][:, 0:256], AF.Copy),
                                tlast, v_tok[t])
                            v_tok[t] = tv
                            rset(("pk", bank), tv)
                            if debug and t == 0 and nb == 11:
                                tdv = B.dma("act", d_dbg, lambda h, t=t: h.dma_start(
                                    dbg["d_v"], v_v[:, t, :]), tv)
                                v_tok[t] = tdv
                if debug and qtiles[0] == 12:
                    kd = [kt_tok.get((0, t2)) for t2 in range(NT)]
                    tdk = B.dma("act", d_dbg, lambda h: h.dma_start(
                        dbg["d_kt"], kt_v[:, 0, :]), *kd)
                    for t2 in range(NT):
                        radd(("kt", 0, t2), tdk)
                rms_war = [wblk_tok[0][-1] if wblk_tok[0] else None,
                           wblk_tok[1][-1] if wblk_tok[1] else None]
                for kc in range(16):
                    for ti in range(4):
                        radd(("rmsT", kc, ti), *rms_war)

                for rr_i in range(2 if LVL >= 5 else 0):
                    rtiles = qtiles[rr_i * 2: rr_i * 2 + 2]
                    a1v = {0: ssat, 1: y2_sc[:, 1, :]}
                    ss_bufs = [ssat, y2_sc[:, 1, :]]
                    pb_bufs = [arb[:, 26624:28672], arb[:, 20480:22528]]
                    SVC = [(7, 8, 9), (26, 27, 28)]

                    def emit_scores(t, mt, hh, L, nch):
                        par = hh % 2
                        ssv = ss_bufs[hh % 2]
                        g_ = hh // 4
                        gpair = g_ // 2
                        base = (g_ % 2) * 64
                        col = (hh // 8) * 4 + (hh % 4)
                        lhs = qt_v[base:base + 64, col, :]
                        sspar = hh % 2
                        nm, dn, rd = SVC[par]
                        kdeps = [kt_tok.get((gpair, t2)) for t2 in range(t + 1)]
                        sc_toks = []
                        for c in range(nch):
                            c0, c1 = c * 512, min(L, (c + 1) * 512)
                            tmm = B.op("pe", lambda h, c0=c0, c1=c1, lhs=lhs,
                                       gpair=gpair, base=base, c=c: h.matmul(
                                pk[c][:, 0:c1 - c0], lhsT=lhs,
                                rhs=kt_v[base:base + 64, gpair, c0:c1],
                                start=True, stop=True),
                                *([kdeps[-1]] + kdeps + rr(("pk", c)) +
                                  rr(("qt", col, base))))
                            radd(("qt", col, base), tmm)
                            sc_toks.append(tmm)
                        ev_toks = []
                        for c in range(nch):
                            c0, c1 = c * 512, min(L, (c + 1) * 512)
                            re_ = c1 if c1 < L else L - 128
                            if re_ > c0:
                                te = B.op("dve", lambda h, c0=c0, re=re_, c=c,
                                          ssv=ssv: h.tensor_copy(
                                    ssv[:, c0:re], pk[c][:, 0:re - c0]),
                                    sc_toks[c], *rr(("ss", sspar)))
                                ev_toks.append(te)
                            if c1 == L:
                                te = B.op("dve", lambda h, c0=c0, c1=c1, c=c, L=L,
                                          ssv=ssv: h.tensor_tensor(
                                    ssv[:, L - 128:L],
                                    pk[c][:, L - 128 - c0:c1 - c0],
                                    mask[:], Op.add),
                                    sc_toks[c], t_mask, *rr(("ss", sspar)))
                                ev_toks.append(te)
                            rset(("pk", c), ev_toks[-1])
                        rset(("ss", sspar), *ev_toks)
                        tmx = B.op("dve", lambda h, L=L, ssv=ssv, nm=nm: h.tensor_reduce(
                            svec[:, nm:nm + 1], ssv[:, 0:L], axis=AX.X, op=Op.max,
                            negate=True), *(ev_toks + rr(("sv", nm))))
                        rset(("sv", nm), tmx)
                        radd(("ss", sspar), tmx)
                        texp = B.op("act", lambda h, L=L, ssv=ssv, par=par, nm=nm,
                                    dn=dn: h.activation(
                            pb_bufs[par][:, 0:L], ssv[:, 0:L], AF.Exp,
                            bias=svec[:, nm:nm + 1], accum_out=svec[:, dn:dn + 1]),
                            tmx, *(rr(("pb", par)) + rr(("sv", dn)) + ev_toks))
                        radd(("sv", nm), texp)
                        rset(("sv", dn), texp)
                        rset(("pb", par), texp)
                        radd(("ss", sspar), texp)
                        trd = B.op("dve", lambda h, dn=dn, rd=rd: h.reciprocal(
                            svec[:, rd:rd + 1], svec[:, dn:dn + 1]), texp,
                            *rr(("sv", rd)))
                        radd(("sv", dn), trd)
                        rset(("sv", rd), trd)
                        tsc = B.op("gp", lambda h, L=L, par=par, rd=rd: h.tensor_scalar(
                            pb_bufs[par][:, 0:L], pb_bufs[par][:, 0:L],
                            svec[:, rd:rd + 1], 0.0, Op.mult, Op.add),
                            texp, trd, *rr(("pb", par)))
                        radd(("sv", rd), tsc)
                        rset(("pb", par), tsc)
                        if debug and t == 1 and hh == 0:
                            tds = B.dma("act", d_dbg, lambda h, ssv=ssv: h.dma_start(
                                dbg["d_ss"], ssv[:, 0:512]), tmx, texp)
                            radd(("ss", sspar), tds)
                            tdp = B.dma("act", d_dbg, lambda h, par=par: h.dma_start(
                                dbg["d_probs"], pb_bufs[par][:, 0:512]), texp)
                            radd(("pb", par), tdp)
                        return dict(t=t, mt=mt, hh=hh, L=L, par=par, g_=g_,
                                    texp=tsc, trd=trd, rd=rd, col=col, base=base)

                    def emit_av(st):
                        nonlocal po_tok
                        par = st["par"]
                        pbv = pb_bufs[par]
                        t = st["t"]
                        mt = st["mt"]
                        hh = st["hh"]
                        obase = (hh % 2) * 64
                        hl = []
                        for kc in range(t + 1):
                            hl.append(tp_put(
                                pbv[:, kc * 128:(kc + 1) * 128],
                                pbv[:, kc * 128:(kc + 1) * 128], 0, 128,
                                "act",
                                [st["texp"]], rr(("pb", par))))
                            radd(("pb", par), hl[-1][0])
                        tp_flush()
                        mm_tok = None
                        for kc in range(t + 1):
                            g_ = st["g_"]
                            mm_tok = B.op("pe", lambda h, kc=kc, g_=g_, t=t, pbv=pbv,
                                          obase=obase: h.matmul(
                                po[obase:obase + 64, 0:128],
                                lhsT=v_v[:, kc, g_ * 64:(g_ + 1) * 64],
                                rhs=pbv[:, kc * 128:(kc + 1) * 128],
                                start=(kc == 0), stop=(kc == t)),
                                hl[kc][1], v_tok[kc], po_tok)
                            radd(("pb", par), mm_tok)
                            if kc == t:
                                v_rd.append(mm_tok)
                        po_tok = mm_tok
                        if hh % 2 == 1:
                            kc2 = hh // 2
                            tev = B.op("act", lambda h, mt=mt, kc2=kc2: h.activation(
                                ot_v[:, mt, kc2, :], po[:, 0:128], AF.Copy),
                                mm_tok, *rr(("ot", mt, kc2)))
                            po_tok = tev
                            rset(("ot", mt, kc2), tev)
                        return rr(("qt", st["col"], st["base"]))[-1]

                    for mt, t in enumerate(rtiles):
                        qhl = []
                        for hh in range(NH):
                            col = (hh // 8) * 4 + (hh % 4)
                            base = ((hh // 4) % 2) * 64
                            qhl.append((col, base, tp_put(
                                qx1[:, t, hh * 64:(hh + 1) * 64],
                                qt_v[base:base + 64, col, :], base, 64, "act",
                                [q_tok[t], x1_tok[t]] + (rms_war if hh == 0 else []),
                                rr(("qt", col, base)))))
                        tp_flush()
                        for col, base, hld in qhl:
                            rset(("qt", col, base), hld[1])
                        last_q_rd = None
                        L = (t + 1) * 128
                        nch = (L + 511) // 512
                        pend = []
                        for hh in range(NH):
                            stA = emit_scores(t, mt, hh, L, nch)
                            pend.append(stA)
                            if len(pend) > 1:
                                last_q_rd = emit_av(pend.pop(0))
                            if qq == 0 and half == 0:
                                pump_mean(3)
                            else:
                                pump_tern(1)
                        while pend:
                            last_q_rd = emit_av(pend.pop(0))
                        qrd_tok[t] = [last_q_rd]
                        if debug and t == 0:
                            od = [rr(("ot", 0, k2))[-1] for k2 in range(16)]
                            tdo = B.dma("act", d_dbg, lambda h: h.dma_start(
                                dbg["d_o"], ot_v[:, 0].rearrange("p a b -> p (a b)")),
                                *od)
                            for k2 in range(16):
                                radd(("ot", 0, k2), tdo)
                    xld = {}
                    for mt, t in enumerate(rtiles):
                        sl = t % 2
                        xld[t] = B.dma("sp", d_x[sl], lambda h, sl=sl, t=t: h.dma_start(
                            xs[:, sl, :], x_h[t * 128:(t + 1) * 128, :]), *xs_tok[sl])
                        xs_tok[sl] = []
                    for nb in range(NBO):
                        gsl = nb % 2
                        twb = B.dma("gp", d_wb[gsl], lambda h, nb=nb, gsl=gsl: h.dma_start(
                            wblk[:, gsl], wo_h[nb].rearrange("(k p) c -> p k c", p=128)),
                            *(wblk_tok[gsl] + (rr(("pb", 1)) if gsl == 1 else [])))
                        wblk_tok[gsl] = []
                        for mt, t in enumerate(rtiles):
                            bank = (nb % 2) * 2 + mt
                            tlast = None
                            for kc in range(16):
                                tlast = B.op("pe", lambda h, kc=kc, mt=mt, bank=bank,
                                             gsl=gsl: h.matmul(
                                    pk[bank][:, 0:256], lhsT=ot_v[:, mt, kc, :],
                                    rhs=wblk[:, gsl, kc, :],
                                    start=(kc == 0), stop=(kc == 15)),
                                    twb, *(rr(("ot", mt, kc)) +
                                           (rr(("pk", bank)) if kc == 0 else [])))
                                radd(("ot", mt, kc), tlast)
                            wblk_tok[gsl].append(tlast)
                            if gsl == 1:
                                radd(("pb", 1), tlast)
                            te = B.op("dve", lambda h, nb=nb, mt=mt, t=t, bank=bank:
                                h.scalar_tensor_tensor(
                                    a1v[mt][:, nb * 256:(nb + 1) * 256],
                                    xs[:, t % 2, nb * 256:(nb + 1) * 256],
                                    r1_all[:, t:t + 1], pk[bank][:, 0:256],
                                    Op.mult, Op.add),
                                tlast, xld[t], r1_tok[t],
                                *rr(("ss", mt)))
                            rset(("pk", bank), te)
                            radd(("ss", mt), te)
                            radd(("r1w", t), te)
                            xs_tok[t % 2].append(te)
                    for mt, t in enumerate(rtiles):
                        tst = stats2048(a1v[mt], *rr(("ss", mt)))
                        tr2 = recip_sqrt(11, bn2[:, 1:2], 1e-5, tst)
                        tt1 = B.op("dve", lambda h, mt=mt: h.tensor_scalar(
                            a1v[mt][:, :], a1v[mt][:, :], bn2[:, 0:1], svec[:, 11:12],
                            Op.subtract, Op.mult), tst, tr2)
                        radd("bn2", tt1)
                        radd(("sv", 11), tt1)
                        tx1 = B.op("dve", lambda h, mt=mt, t=t: h.tensor_tensor(
                            qx1[:, t, :], a1v[mt][:, :], xs[:, t % 2, :], Op.add),
                            tt1, xld[t], q_tok[t], *qrd_tok[t])
                        xs_tok[t % 2].append(tx1)
                        x1_tok[t] = tx1
                        rset(("ss", mt), tx1)
                        if debug and t == 0:
                            tdx = B.dma("act", d_dbg, lambda h, t=t: h.dma_start(
                                dbg["d_x1"], qx1[:, t, :]), tx1)
                            x1_tok[t] = tdx
                arb_hi_war = [wblk_tok[0][-1] if wblk_tok[0] else None,
                              wblk_tok[1][-1] if wblk_tok[1] else None, po_tok]

        fin_mean()
        pump_tern(10 ** 9)
        p3_done = ([wblk_tok[0][-1]] if wblk_tok[0] else []) + \
                  ([wblk_tok[1][-1]] if wblk_tok[1] else []) + \
                  [po_tok] + x1_tok + rr(("pb", 0)) + rr(("pb", 1)) + \
                  rr(("pb", 2)) + rr(("ss", 0)) + rr(("ss", 1)) + v_rd[-4:]
        sa_col = [18, 19]
        sy_col = [20, 21]
        aq_rd = [[], []]
        gq_war = arb_hi_war + rr(("pb", 0))
        for _sl in range(2):
            radd(("w3", _sl), *p3_done)

        def quant_chain(src_ap, width, s_col, dst_bf, dep, war):
            toks = []
            for cc in range(width // 512):
                sv = src_ap[:, cc * 512:(cc + 1) * 512]
                dv = dst_bf[:, cc * 512:(cc + 1) * 512]
                dsc = y2_sc[:, 0, 0:512]
                csc = y2_sc[:, 0, 1024:1536]
                tq0 = B.op("act", lambda h, sv=sv, s_col=s_col: h.activation(
                    q32[:, :], sv, AF.Copy, scale=svec[:, s_col:s_col + 1]),
                    dep, *(rr("q32") + rr(("sv", s_col))))
                radd(("sv", s_col), tq0)
                td = B.op("dve", lambda h, sv=sv, s_col=s_col: h.scalar_tensor_tensor(
                    dsc, sv, svec[:, s_col:s_col + 1], q32[:, :],
                    Op.mult, Op.subtract), tq0, *rr("y2a"))
                radd(("sv", s_col), td)
                tc = B.op("gp", lambda h: h.tensor_scalar(
                    csc, dsc, 0.5, -0.5, Op.min, Op.max), td)
                tv2 = B.op("dve", lambda h: h.tensor_tensor(
                    dsc, dsc, csc, Op.subtract), tc)
                tsg = B.op("act", lambda h: h.activation(csc, dsc, AF.Sign), tv2, tq0)
                rset("y2a", tsg)
                ta = B.op("dve", lambda h, dv=dv: h.tensor_tensor(
                    dv, q32[:, :], csc, Op.add), tsg,
                    *(war if cc == 0 else []))
                radd("y2a", ta)
                rset("q32", ta)
                toks.append(ta)
            return toks

        down_last = None
        def s1_chains(gi):
            gtiles = [2 * gi, 2 * gi + 1]
            out = []
            for mt, t in enumerate(gtiles):
                tst = stats2048(qx1[:, t, :], x1_tok[t])
                tr3 = recip_sqrt(12, bn2[:, 1:2], 1e-5, tst)
                tln = B.op("dve", lambda h, t=t: h.tensor_scalar(
                    ssat[:, :], qx1[:, t, :], bn2[:, 0:1], svec[:, 12:13],
                    Op.subtract, Op.mult), tst, tr3,
                    *(rr(("ss", 0)) + (p3_done if gi == 0 and mt == 0 else [])))
                radd("bn2", tln)
                radd(("sv", 12), tln)
                rset(("ss", 0), tln)
                tmx2 = B.op("dve", lambda h: h.tensor_reduce(
                    svec[:, 14:15], ssat[:, :], axis=AX.X, op=Op.max,
                    apply_absolute_value=True), tln, *rr(("sv", 14)))
                radd(("ss", 0), tmx2)
                tmx2 = B.op("dve", lambda h: h.tensor_scalar(
                    svec[:, 14:15], svec[:, 14:15], 1e-5, None, Op.max), tmx2)
                rset(("sv", 14), tmx2)
                trc = B.op("dve", lambda h: h.reciprocal(svec[:, 15:16], svec[:, 14:15]),
                           tmx2, *rr(("sv", 15)))
                radd(("sv", 14), trc)
                rset(("sv", 15), trc)
                ts1 = B.op("dve", lambda h: h.tensor_scalar(
                    svec[:, 16:17], svec[:, 15:16], 127.0, None, Op.mult), trc,
                    *rr(("sv", 16)))
                radd(("sv", 15), ts1)
                rset(("sv", 16), ts1)
                tis = B.op("dve", lambda h: h.reciprocal(svec[:, 17:18], svec[:, 16:17]),
                           ts1, *rr(("sv", 17)))
                radd(("sv", 16), tis)
                rset(("sv", 17), tis)
                tsa = B.op("dve", lambda h, c=sa_col[mt]: h.tensor_tensor(
                    svec[:, c:c + 1], svec[:, 17:18], sca[:, 4:5], Op.mult), tis,
                    sca_hold[0], *rr(("sv", sa_col[mt])))
                radd(("sv", 17), tsa)
                rset(("sv", sa_col[mt]), tsa)
                war = aq_rd[mt] + (p3_done if gi == 0 and mt == 0 else [])
                aq_rd[mt] = []
                aqs = quant_chain(ssat, D, 16, aq_bufs[mt], tln, war)
                radd(("ss", 0), *aqs)
                if debug and gi == 0 and mt == 0:
                    tda = B.dma("act", d_dbg, lambda h: h.dma_start(
                        dbg["d_aq"], aq_bufs[0]), *aqs)
                    aq_rd[mt].append(tda)
                out.append(aqs)
            return out

        def s1_tp(gi, aqs2):
            for mt in range(2):
                ahl = []
                for kc in range(16):
                    ahl.append(tp_put(
                        aq_bufs[mt][:, kc * 128:(kc + 1) * 128], aqT_v[:, mt, kc, :],
                        0, 128, "act", [aqs2[mt][kc // 4]], rr(("aqT", mt, kc))))
                    aq_rd[mt].append(ahl[-1][0])
                tp_flush()
                for kc in range(16):
                    rset(("aqT", mt, kc), ahl[kc][1])
                aqT_done[mt] = ahl[-1][1]

        aqT_done = [None, None]
        nxt_aqs = s1_chains(0) if LVL >= 6 else None
        for gi in range(8 if LVL >= 6 else 0):
            gtiles = [2 * gi, 2 * gi + 1]
            s1_tp(gi, nxt_aqs)
            g_done = [None, None]
            last_mm = {}
            for j in range(16):
                for half_ in range(2):
                    blkid = 2 * j + half_
                    for qq4 in range(2):
                        sl = (blkid * 2 + qq4) % 2
                        tld = B.dma("sp", d_w3[sl], lambda h, blkid=blkid, qq4=qq4, sl=sl:
                            h.dma_start(w8slots[sl],
                                        sgu_h[blkid, :, qq4 * 4096:(qq4 + 1) * 4096]),
                            *((tern_gu[blkid] or []) + rr(("w3", sl))))
                        rset(("w3", sl), None)
                        for mt in range(2):
                            bank = half_ * 2 + mt
                            for k4 in range(8):
                                kc = qq4 * 8 + k4
                                tmm = B.op("pe", lambda h, mt=mt, kc=kc, k4=k4, sl=sl,
                                           bank=bank: h.matmul(
                                    pk[bank][:, 0:512], lhsT=aqT_v[:, mt, kc, :],
                                    rhs=w8slots[sl][:, k4 * 512:(k4 + 1) * 512],
                                    start=(kc == 0), stop=(kc == 15)),
                                    tld, *(rr(("aqT", mt, kc)) +
                                           ([aqT_done[mt]] if kc == 0 else []) +
                                           (rr(("pk", bank)) if kc == 0 else [])))
                                radd(("aqT", mt, kc), tmm)
                                radd(("w3", sl), tmm)
                                last_mm[bank] = tmm
                for mt in range(2):
                    gb, ub = mt, 2 + mt
                    tsl = B.op("act", lambda h, gb=gb, mt=mt: h.activation(
                        sil_sc[:, :], pk[gb][:, 0:512], AF.Silu,
                        scale=svec[:, sa_col[mt]:sa_col[mt] + 1]),
                        last_mm[gb],
                        *(rr("sil") + rr(("sv", sa_col[mt])) +
                          (rr(("stg", 0)) if (gi == 0 and j == 0) else [])))
                    radd(("sv", sa_col[mt]), tsl)
                    rset("sil", tsl)
                    tgc = B.op("dve", lambda h, ub=ub, mt=mt, j=j: h.scalar_tensor_tensor(
                        g_v[:, mt, j * 512:(j + 1) * 512], pk[ub][:, 0:512],
                        svec[:, sa_col[mt]:sa_col[mt] + 1], sil_sc[:, :],
                        Op.mult, Op.mult), tsl, g_done[mt], last_mm[ub],
                        *(p3_done if (gi == 0 and j == 0) else []))
                    radd(("sv", sa_col[mt]), tgc)
                    radd("sil", tgc)
                    rset(("pk", gb), tgc)
                    rset(("pk", ub), tgc)
                    g_done[mt] = tgc
            if debug and gi == 0:
                tdg = B.dma("act", d_dbg, lambda h: h.dma_start(dbg["d_g"], g_v[:, 0, :]),
                      g_done[0])
                g_done[0] = tdg
            gqs2 = []
            tsy2 = []
            for mt, t in enumerate(gtiles):
                tmg = B.op("dve", lambda h, mt=mt: h.tensor_reduce(
                    svec[:, 22:23], g_v[:, mt, :], axis=AX.X, op=Op.max,
                    apply_absolute_value=True), g_done[mt], *rr(("sv", 22)))
                tmg = B.op("dve", lambda h: h.tensor_scalar(
                    svec[:, 22:23], svec[:, 22:23], 1e-5, None, Op.max), tmg)
                rset(("sv", 22), tmg)
                trc = B.op("dve", lambda h: h.reciprocal(svec[:, 23:24], svec[:, 22:23]),
                           tmg, *rr(("sv", 23)))
                radd(("sv", 22), trc)
                rset(("sv", 23), trc)
                ts2 = B.op("dve", lambda h: h.tensor_scalar(
                    svec[:, 24:25], svec[:, 23:24], 127.0, None, Op.mult), trc,
                    *rr(("sv", 24)))
                radd(("sv", 23), ts2)
                rset(("sv", 24), ts2)
                tis2 = B.op("dve", lambda h: h.reciprocal(svec[:, 25:26], svec[:, 24:25]),
                            ts2, *rr(("sv", 25)))
                radd(("sv", 24), tis2)
                rset(("sv", 25), tis2)
                tsy = B.op("dve", lambda h, c=sy_col[mt]: h.tensor_tensor(
                    svec[:, c:c + 1], svec[:, 25:26], sca[:, 5:6], Op.mult), tis2,
                    sca_hold[0], *rr(("sv", sy_col[mt])))
                radd(("sv", 25), tsy)
                rset(("sv", sy_col[mt]), tsy)
                war_gq = gq_war if (gi == 0 and mt == 0) else \
                    ([down_last] if down_last else [])
                gqs = quant_chain(g_v[:, mt, :], FF, 24, g_v[:, mt, :], g_done[mt],
                                  war_gq)
                if debug and gi == 0 and mt == 0:
                    tdgq = B.dma("act", d_dbg, lambda h: h.dma_start(dbg["d_gq"], g_v[:, 0, :]),
                          *gqs)
                    gqs = gqs[:-1] + [tdgq]
                gqs2.append(gqs)
                tsy2.append(tsy)
            if gi < 7 and LVL >= 6:
                nxt_aqs = s1_chains(gi + 1)
            for mt, t in enumerate(gtiles):
                gqs = gqs2[mt]
                tsy = tsy2[mt]
                for kcb in range(8):
                    dhl = []
                    for k in range(8):
                        kc = kcb * 8 + k
                        dhl.append(tp_put(
                            g_v[:, mt, kc * 128:(kc + 1) * 128], gqt_v8[:, k, :],
                            0, 128, "act", [gqs[kc // 4]], rr(("gqt8", k))))
                    tp_flush()
                    for kp in range(4):
                        kc0 = kcb * 8 + kp * 2
                        sl = (kcb * 4 + kp) % 2
                        tld = B.dma("sp", d_w3[sl], lambda h, kc0=kc0, sl=sl: h.dma_start(
                            w8slots[sl].rearrange("p (a c) -> p a c", a=2),
                            sdo_h[kc0:kc0 + 2].rearrange("a p c -> p a c")),
                            *((tern_do[kc0] or []) + (tern_do[kc0 + 1] or []) +
                              rr(("w3", sl))))
                        rset(("w3", sl), None)
                        for ki in range(2):
                            kc = kc0 + ki
                            k = kp * 2 + ki
                            rset(("gqt8", k), None)
                            for n in range(4):
                                tmm = B.op("pe", lambda h, kc=kc, sl=sl, ki=ki, k=k, n=n:
                                    h.matmul(
                                    pk[n][:, 0:512], lhsT=gqt_v8[:, k, :],
                                    rhs=w8slots[sl][:, ki * 2048 + n * 512:
                                                    ki * 2048 + (n + 1) * 512],
                                    start=(kc == 0), stop=(kc == 63)),
                                    dhl[k][1], tld,
                                    *(rr(("pk", n)) if kc == 0 else []))
                                radd(("gqt8", k), tmm)
                                radd(("w3", sl), tmm)
                                down_last = tmm
                ev = None
                evs = []
                for n in range(4):
                    ev = B.op("dve", lambda h, n=n, mt=mt: h.tensor_scalar(
                        y2_sc[:, 1, n * 512:(n + 1) * 512], pk[n][:, 0:512],
                        svec[:, sy_col[mt]:sy_col[mt] + 1], None, Op.mult),
                        down_last, tsy, *(rr("y2b") if n == 0 else []))
                    rset(("pk", n), ev)
                    radd(("sv", sy_col[mt]), ev)
                    evs.append(ev)
                rset("y2b", *evs)
                if debug and gi == 0 and mt == 0:
                    tdy = B.dma("act", d_dbg, lambda h: h.dma_start(
                        dbg["d_y2"], y2_sc[:, 1, :]), *evs)
                    radd("y2b", tdy)
                tst = stats2048(y2_sc[:, 1, :], *evs)
                tr4 = recip_sqrt(13, bn2[:, 1:2], 1e-5, tst)
                tt2 = B.op("dve", lambda h: h.tensor_scalar(
                    y2_sc[:, 1, :], y2_sc[:, 1, :], bn2[:, 0:1], svec[:, 13:14],
                    Op.subtract, Op.mult), tst, tr4, *(list(evs) + rr("y2b")))
                radd("bn2", tt2)
                radd(("sv", 13), tt2)
                tout = B.op("dve", lambda h, t=t: h.tensor_tensor(
                    y2_sc[:, 0, :], y2_sc[:, 1, :], qx1[:, t, :], Op.add), tt2,
                    x1_tok[t], *rr("y2a"))
                rset("y2b", tout)
                rset("y2a", tout)
                tst2 = B.dma("sp", d_out, lambda h, t=t: h.dma_start(
                    out_h[t * 128:(t + 1) * 128, :], y2_sc[:, 0, :]), tout)
                radd("y2a", tst2)
                out_toks.append(tst2)

        B.raw("act", lambda h: h.wait_ge(d_out.h, d_out.n))
        if debug:
            B.raw("act", lambda h: h.wait_ge(d_dbg.h, d_dbg.n))
        for dsm in [d_x[0], d_x[1], d_wb[0], d_wb[1], d_st[0], d_st[1],
                    d_t8[0], d_t8[1], d_sg[0], d_sg[1], d_sd[0], d_sd[1], d_misc]:
            B.raw("act", lambda h, dsm=dsm: h.wait_ge(dsm.h, dsm.n))
        B.flush(block)
    return nc, dbg


def _prep(w_in, w_out, rms_w, w_gu, w_down):
    wi = np.ascontiguousarray(w_in.T.reshape(D, NBI, 256).transpose(1, 0, 2))
    wo = np.ascontiguousarray(w_out.T.reshape(D, NBO, 256).transpose(1, 0, 2))
    rw = np.ascontiguousarray(np.asarray(rms_w).reshape(1, D))
    gu = np.ascontiguousarray(
        w_gu.T.reshape(D, 2, 16, 512).transpose(2, 1, 0, 3).reshape(32, D, 512))
    wd = np.ascontiguousarray(w_down.T)
    return wi, wo, rw, gu, wd


def kernel(L0_attn_x, L0_attn_w_in, L0_attn_w_out, L0_attn_rms_w,
           L0_ffn_w_gate_up, L0_ffn_w_down):
    x = np.ascontiguousarray(np.asarray(L0_attn_x, dtype=np.float32))
    wi, wo, rw, gu, wd = _prep(
        np.asarray(L0_attn_w_in, np.float32), np.asarray(L0_attn_w_out, np.float32),
        np.asarray(L0_attn_rms_w, np.float32),
        np.asarray(L0_ffn_w_gate_up, np.float32),
        np.asarray(L0_ffn_w_down, np.float32))
    nc, _ = build_program(debug=False)
    in_maps = [{"x": np.ascontiguousarray(x[b]), "w_in_blk": wi, "w_out_blk": wo,
                "rms_w": rw, "w_gu_pairs": gu, "w_do_T": wd} for b in range(8)]
    res = run_bass_kernel_spmd(nc, in_maps, core_ids=list(range(8)))
    return np.stack([np.asarray(r["out"], np.float32) for r in res.results], axis=0)


# version 17
# speedup vs baseline: 1.0810x; 1.0810x over previous
import os
import numpy as np
from contextlib import ExitStack

import concourse.bass as bass
import concourse.mybir as mybir
from concourse.bass_utils import run_bass_kernel_spmd
from concourse.alu_op_type import AluOpType as Op

F32 = mybir.dt.float32
BF16 = mybir.dt.bfloat16
I32 = mybir.dt.int32
FP8 = mybir.dt.float8e4
AF = mybir.ActivationFunctionType
AX = mybir.AxisListType

S, D = 2048, 2048
NT = 16
NH, DH = 32, 64
FF = 8192
NBI, NBO = 12, 8
THETA = 10000.0

_2PI = 2.0 * np.pi
C1 = 6.28125
C2 = float((np.float32(_2PI - C1).view(np.uint32) & np.uint32(0xFFFFF000)).view(np.float32))
C3 = float(np.float32(_2PI - C1 - C2))
INV2PI = float(np.float32(1.0 / _2PI))
HPI = float(np.float32(np.pi / 2))
PCLIP = float(np.nextafter(np.float32(np.pi), np.float32(0)))
TWO_PI32 = float(np.float32(2.0 * np.pi))
NEG_PI32 = float(-np.float32(np.pi))
LNT32 = float(np.float32(-np.log(THETA) / 32.0))


class Sem:
    __slots__ = ("h", "n", "name", "chain")

    def __init__(self, ctx, nc, name):
        self.h = ctx.enter_context(nc.semaphore(name))
        self.n = 0
        self.name = name


class Build:
    ENGS = ("sp", "gp", "pe", "dve", "act")

    def __init__(self, nc, ctx):
        self.nc = nc
        self.qs = {e: [] for e in self.ENGS}
        self.es = {e: Sem(ctx, nc, "es_" + e) for e in ("pe", "dve", "act", "gp")}
        self.wm = {e: {} for e in self.ENGS}
        self.ctx = ctx

    def sem(self, name):
        return Sem(self.ctx, self.nc, name)

    def _push(self, eng, deps, emit, incs):
        dd = {}
        wm = self.wm[eng]
        own = self.es.get(eng)
        for t in deps:
            if t is None:
                continue
            s, v = t
            if wm.get(s, -1) >= v:
                continue
            if s not in dd or dd[s] < v:
                dd[s] = v
        for s, v in dd.items():
            wm[s] = v
        deps = [(s, v) for s, v in dd.items()]

        def run(h, deps=deps, emit=emit, incs=incs):
            for s, v in deps[1:]:
                h.wait_ge(s.h, v)
            ins = emit(h)
            if deps:
                ins._wait_ge(deps[0][0].h, deps[0][1])
            for s, amt in incs:
                ins.then_inc(s.h, amt)

        self.qs[eng].append(run)

    def op(self, eng, emit, *deps):
        s = self.es[eng]
        s.n += 1
        self._push(eng, deps, emit, [(s, 1)])
        return (s, s.n)

    def dma(self, eng, dsem, emit, *deps):
        prev = (dsem, dsem.n) if getattr(dsem, "chain", False) and dsem.n else None
        deps = list(deps) + [prev]
        if getattr(dsem, "chain", False):
            deps += [(s, s.n) for s in self.es.values() if s.n]
        dsem.n += 16
        self._push(eng, deps, emit, [(dsem, 16)])
        return (dsem, dsem.n)

    def raw(self, eng, fn):
        self.qs[eng].append(fn)

    def flush(self, block):
        mp = dict(pe=block.tensor, dve=block.vector, act=block.scalar,
                  gp=block.gpsimd, sp=block.sync)
        for e in self.ENGS:
            if self.qs[e]:
                fns = self.qs[e]
                self.qs[e] = []
                mp[e](lambda h, fns=fns: [f(h) for f in fns])


def build_program(debug=False, upto="full"):
    nc = bass.Bass("TRN2", target_bir_lowering=False)
    x_h = nc.dram_tensor("x", [S, D], F32, kind="ExternalInput").ap()
    wi_h = nc.dram_tensor("w_in_blk", [NBI, D, 256], F32, kind="ExternalInput").ap()
    wo_h = nc.dram_tensor("w_out_blk", [NBO, D, 256], F32, kind="ExternalInput").ap()
    rw_h = nc.dram_tensor("rms_w", [1, D], F32, kind="ExternalInput").ap()
    wgu_h = nc.dram_tensor("w_gu_pairs", [32, D, 512], F32, kind="ExternalInput").ap()
    wdo_h = nc.dram_tensor("w_do_T", [FF, D], F32, kind="ExternalInput").ap()
    out_h = nc.dram_tensor("out", [S, D], F32, kind="ExternalOutput").ap()
    sgu_h = nc.dram_tensor("scr_gu", [32, 128, 16 * 512], FP8, kind="Internal").ap()
    sdo_h = nc.dram_tensor("scr_do", [64, 128, D], FP8, kind="Internal").ap()

    dbg = {}
    if debug:
        for nm, shp, dt in [
            ("d_rms", [128, D], BF16), ("d_q", [128, D], BF16),
            ("d_kt", [128, S], BF16), ("d_v", [128, 512], BF16),
            ("d_ss", [128, 512], F32), ("d_probs", [128, 512], BF16),
            ("d_o", [128, D], BF16), ("d_x1", [128, D], BF16),
            ("d_sc", [1, 6], F32), ("d_tabs", [128, 2048], BF16), ("d_ang", [128, 1024], F32),
            ("d_g", [128, FF], BF16), ("d_gq", [128, FF], BF16),
            ("d_y2", [128, D], F32), ("d_aq", [128, D], BF16),
        ]:
            dbg[nm] = nc.dram_tensor(nm, shp, dt, kind="ExternalOutput").ap()

    ctx = ExitStack()
    with ctx:
        sb = lambda name, shape, dt=F32: ctx.enter_context(nc.sbuf_tensor(name, shape, dt))
        psum = lambda name, shape, dt=F32: ctx.enter_context(nc.psum_tensor(name, shape, dt))
        B = Build(nc, ctx)

        ident_b = sb("ident_b", [128, 128], BF16)
        mask = sb("mask", [128, 128], F32)
        ones_f = sb("ones_f", [128, 128], F32)
        onecol = sb("onecol", [128, 1], F32)
        svec = sb("svec", [128, 40], F32)
        r1_all = sb("r1_all", [128, NT], F32)
        bnb = sb("bnb", [128, 24], F32)
        bn2 = sb("bn2", [128, 2], F32)
        sca = sb("sca", [128, 8], F32)
        acc_gu = sb("acc_gu", [128, 256], F32)
        acc_do = sb("acc_do", [128, 128], F32)
        acc1 = sb("acc1", [1, 8], F32)
        tabs = sb("tabs", [128, 2, NT * 64], BF16)
        rms_w_bc = sb("rms_w_bc", [128, D], F32)
        xs = sb("xs", [128, 2, D], F32)
        stage = sb("stage", [128, 2, 1024], F32)
        st8 = sb("st8", [128, 2, 1024], FP8)
        sgu_rd = sb("sgu_rd", [128, 2, 2048], FP8)
        q32 = sb("q32", [128, 1024], I32)
        qx1 = sb("qx1", [128, NT, D], BF16)
        arb = sb("arb", [128, 32768], BF16)
        ssat = sb("ssat", [128, D], F32)
        y2_sc = sb("y2_sc", [128, 2, D], F32)
        sil_sc = stage[:, 0, 0:512]
        w8s = sb("w8s", [128, 2, 2048], FP8)
        opair = sb("opair", [128, 128], BF16)
        pts = sb("pts", [128, 2, 128], BF16)
        gqt = sb("gqt", [128, 2, 128], BF16)

        kt_v = arb[:, 0:8192].rearrange("p (g s) -> p g s", g=4)
        v_v = arb[:, 8192:16384].rearrange("p (t c) -> p t c", t=16)
        wblk = arb[:, 16384:24576].rearrange("p (s k c) -> p s k c", s=2, k=16)
        rmsT = arb[:, 24576:32768].rearrange("p (k c) -> p k c", k=16)
        qt_v = arb[:, 24576:26624].rearrange("p (k c) -> p k c", k=16)
        pb_off = 26624
        ot_v = arb[:, 28672:32768].rearrange("p (t k c) -> p t k c", t=2, k=16)
        g_v = arb[:, 0:16384].rearrange("p (t f) -> p t f", t=2)
        gqt_v8 = arb[:, 24576:25600].rearrange("p (k c) -> p k c", k=8)
        aqT_v = arb[:, 16384:20480].rearrange("p (t k c) -> p t k c", t=2, k=16)
        aq_st = arb[:, 20480:22528]
        gq_v = arb[:, 24576:32768]

        w8slots = [w8s[:, 0, :], w8s[:, 1, :],
                   st8[:, :, :].rearrange("p a b -> p (a b)")]
        pk = [psum("pk%d" % i, [128, 512], F32) for i in range(4)]
        po = psum("po", [128, 512], F32)
        tpb = [psum("tp%d" % i, [128, 1024], BF16) for i in range(3)]

        d_x = [B.sem("d_x0"), B.sem("d_x1")]
        d_wb = [B.sem("d_wb0"), B.sem("d_wb1")]
        d_st = [B.sem("d_st0"), B.sem("d_st1")]
        d_t8 = [B.sem("d_t80"), B.sem("d_t81")]
        d_sg = [B.sem("d_sg0"), B.sem("d_sg1")]
        d_sd = [B.sem("d_sd0"), B.sem("d_sd1")]
        d_w3 = [B.sem("d_w30"), B.sem("d_w31"), B.sem("d_w32")]
        d_out = B.sem("d_out")
        d_dbg = B.sem("d_dbg")
        d_dbg.chain = True
        d_misc = B.sem("d_misc")

        block = ctx.enter_context(nc.Block())

        R = {}

        def rr(k):
            return R.get(k, [])

        def rset(k, *toks):
            R[k] = [t for t in toks if t is not None]

        def radd(k, *toks):
            R.setdefault(k, []).extend(t for t in toks if t is not None)

        tp_state = {"bank": 0, "items": [], "last_tt": None,
                    "free": [[], [], []]}

        def tp_flush():
            st = tp_state
            if not st["items"]:
                return
            evs = []
            for (hld, pv, dst_ap, base, nrow, eng, dst_war) in st["items"]:
                if eng == "act":
                    tev = B.op("act", lambda h, pv=pv, dst_ap=dst_ap, base=base,
                               nrow=nrow: h.activation(
                        dst_ap, pv[base:base + nrow, :], AF.Copy),
                        st["last_tt"], *dst_war)
                else:
                    tev = B.op("dve", lambda h, pv=pv, dst_ap=dst_ap, base=base,
                               nrow=nrow: h.tensor_copy(
                        dst_ap, pv[base:base + nrow, :]),
                        st["last_tt"], *dst_war)
                hld[1] = tev
                evs.append(tev)
            st["free"][st["bank"]] = evs
            st["bank"] = (st["bank"] + 1) % 3
            st["items"] = []
            st["last_tt"] = None

        def tp_put(src_ap, dst_ap, base, nrow, eng, pre_deps, dst_war):
            st = tp_state
            if len(st["items"]) >= int(os.environ.get("TPB", "5")):
                tp_flush()
            qq = len(st["items"])
            pv = tpb[st["bank"]][:, qq * 128:(qq + 1) * 128]
            tt = B.op("pe", lambda h, pv=pv, src_ap=src_ap, base=base, nrow=nrow:
                      h.transpose(pv[base:base + nrow, :], src_ap, ident_b[:]),
                      *(list(pre_deps) + (st["free"][st["bank"]] if qq == 0 else [])
                        + [t_idb]))
            st["last_tt"] = tt
            hld = [tt, None]
            st["items"].append((hld, pv, dst_ap, base, nrow, eng, list(dst_war)))
            return hld

        def transpose_ev(src_ap, dst_ap, base, nrow, eng, pre_deps, dst_war):
            hld = tp_put(src_ap, dst_ap, base, nrow, eng, pre_deps, dst_war)
            tp_flush()
            return hld[0], hld[1]

        t_rw = B.dma("sp", d_misc, lambda h: h.dma_start(rms_w_bc[0:1, :], rw_h[:, :]))
        t_on = B.op("gp", lambda h: h.memset(ones_f[:], 1.0))
        t_oc = B.op("gp", lambda h: h.memset(onecol[:], 1.0))
        t_id = B.op("gp", lambda h: h.affine_select(
            mask[:], ones_f[:], pattern=[[-1, 128]], base=0, channel_multiplier=1,
            compare_op=Op.is_equal, fill=0.0), t_on)
        t_idb = B.op("dve", lambda h: h.tensor_copy(ident_b[:], mask[:]), t_id)
        t_m0 = B.op("gp", lambda h: h.memset(mask[:], 0.0), t_idb)
        t_mask = B.op("gp", lambda h: h.affine_select(
            mask[:], mask[:], pattern=[[-1, 128]], base=0, channel_multiplier=1,
            compare_op=Op.is_ge, fill=-1e30), t_m0)

        for c in range(4):
            tmm = B.op("pe", lambda h, c=c: h.matmul(
                pk[c][:, 0:512], lhsT=ones_f[0:1, 0:128],
                rhs=rms_w_bc[0:1, c * 512:(c + 1) * 512], start=True, stop=True),
                t_rw, t_on)
            tev = B.op("dve", lambda h, c=c: h.tensor_copy(
                rms_w_bc[:, c * 512:(c + 1) * 512], pk[c][:, 0:512]), tmm)
            rset(("pk", c), tev)
            radd("rwbc", tev)
        t_rb = rr("rwbc")[-1]

        angv = y2_sc[:, 0, 0:1024]
        a2v = y2_sc[:, 1, 0:1024]
        tgv = ssat[:, 0:1024]
        rrv = ssat[:, 1024:2048]
        t_tg = B.op("gp", lambda h: h.iota(tgv, pattern=[[128, 16], [0, 64]],
                    base=0, channel_multiplier=1, allow_small_or_imprecise_dtypes=True))
        t_eg = B.op("gp", lambda h: h.iota(angv, pattern=[[0, 16], [0, 2], [1, 32]],
                    base=0, channel_multiplier=0, allow_small_or_imprecise_dtypes=True))
        t_if = B.op("act", lambda h: h.activation(a2v, angv, AF.Exp, scale=LNT32), t_eg)
        t_ang = B.op("dve", lambda h: h.tensor_tensor(angv, tgv, a2v, Op.mult), t_if, t_tg)
        t_q = B.op("act", lambda h: h.activation(q32[:, :], angv, AF.Copy,
                   scale=INV2PI, bias=0.5), t_ang)
        t_r = t_q
        for i, cc in enumerate((C1, C2, C3)):
            t_r = B.op("dve", lambda h, cc=cc, i=i: h.scalar_tensor_tensor(
                rrv, q32[:, :], -cc, angv if i == 0 else rrv, Op.mult, Op.add),
                t_r, t_ang)
        t_c = B.op("dve", lambda h: h.tensor_scalar(
            tgv, rrv, NEG_PI32, None, Op.is_lt), t_r, t_ang)
        t_r = B.op("dve", lambda h: h.scalar_tensor_tensor(
            rrv, tgv, TWO_PI32, rrv, Op.mult, Op.add), t_c)
        t_r = B.op("gp", lambda h: h.tensor_scalar(
            rrv, rrv, PCLIP, -PCLIP, Op.min, Op.max), t_r)
        t_sin = B.op("act", lambda h: h.activation(tabs[:, 0, :], rrv, AF.Sin), t_r)
        t_a2 = B.op("dve", lambda h: h.tensor_scalar_add(a2v, angv, HPI), t_r, t_if)
        t_q2 = B.op("act", lambda h: h.activation(q32[:, :], angv, AF.Copy,
                    scale=INV2PI, bias=0.75), t_sin, t_q)
        t_r2 = t_q2
        for i, cc in enumerate((C1, C2, C3)):
            t_r2 = B.op("dve", lambda h, cc=cc, i=i: h.scalar_tensor_tensor(
                rrv, q32[:, :], -cc, a2v if i == 0 else rrv, Op.mult, Op.add),
                t_r2, t_a2, t_sin)
        t_c2 = B.op("dve", lambda h: h.tensor_scalar(
            tgv, rrv, NEG_PI32, None, Op.is_lt), t_r2)
        t_r2 = B.op("dve", lambda h: h.scalar_tensor_tensor(
            rrv, tgv, TWO_PI32, rrv, Op.mult, Op.add), t_c2)
        t_r2 = B.op("gp", lambda h: h.tensor_scalar(
            rrv, rrv, PCLIP, -PCLIP, Op.min, Op.max), t_r2)
        t_cos = B.op("act", lambda h: h.activation(tabs[:, 1, :], rrv, AF.Sin), t_r2)
        rset("y2a", t_r2)
        rset("y2b", t_r2)
        rset(("ss", 0), t_cos)
        rset(("ss", 1), t_r2)
        rset("q32", t_r2)
        rset("tabs0", t_sin)
        rset("tabs1", t_cos)
        if debug:
            t_dt = B.dma("act", d_dbg, lambda h: h.dma_start(
                dbg["d_tabs"], tabs[:, :, :].rearrange("p a b -> p (a b)")), t_sin, t_cos)
            radd("tabs0", t_dt); radd("tabs1", t_dt)
            t_da = B.dma("act", d_dbg, lambda h: h.dma_start(dbg["d_ang"], angv), t_ang, t_q2)
            radd("y2a", t_da)

        PH = ["p0", "mean", "tern", "p1", "qkv", "attn", "ffn", "full"]
        LVL = PH.index(upto) if upto in PH else 6
        gu_flat = wgu_h.rearrange("b d c -> (b d c)")
        do_flat = wdo_h.rearrange("f d -> (f d)")
        U = 128 * 1024

        def mean_unit(i, flat, off, acc, col):
            sl = i % 2
            dst = stage[:, sl, :]
            td = B.dma("sp", d_st[sl], lambda h, dst=dst, flat=flat, off=off:
                       h.dma_start(dst, flat[off:off + U].rearrange("(p c) -> p c", p=128)),
                       *rr(("stg", sl)))
            tr = B.op("dve", lambda h, acc=acc, col=col, dst=dst: h.tensor_reduce(
                acc[:, col:col + 1], dst, axis=AX.X, op=Op.add,
                apply_absolute_value=True), td)
            rset(("stg", sl), tr)
            return tr

        mean_jobs = []
        mean_done = [None, None]
        ui = 0
        for i in range(256 if LVL >= 1 else 0):
            mean_jobs.append((ui, gu_flat, i * U, acc_gu, i, 0)); ui += 1
        for i in range(128 if LVL >= 1 else 0):
            mean_jobs.append((ui, do_flat, i * U, acc_do, i, 1)); ui += 1

        def pump_mean(k):
            while k > 0 and mean_jobs:
                a = mean_jobs.pop(0)
                mean_done[a[5]] = mean_unit(*a[:5])
                k -= 1

        sca_toks = []
        sca_hold = [None]

        def fin_mean():
            nonlocal sca_toks
            if LVL < 1 or sca_hold[0] is not None:
                return
            pump_mean(10 ** 9)
            for (acc, ncol, scale, hcol, scol, dep) in [
                (acc_gu, 256, 2.0 ** -25, 0, 4, mean_done[0]),
                (acc_do, 128, 2.0 ** -24, 2, 5, mean_done[1]),
            ]:
                t1 = B.op("dve", lambda h, acc=acc, n=ncol: h.tensor_reduce(
                    svec[:, 0:1], acc[:, 0:n], axis=AX.X, op=Op.add), dep, *rr("sv0"))
                rset("sv0", t1)
                tm2 = B.op("pe", lambda h: h.matmul(pk[0][0:1, 0:1], lhsT=onecol[:],
                           rhs=svec[:, 0:1], start=True, stop=True), t1, t_oc,
                           *rr(("pk", 0)))
                radd("sv0", tm2)
                t2 = B.op("dve", lambda h: h.tensor_copy(acc1[0:1, 0:1], pk[0][0:1, 0:1]), tm2)
                rset(("pk", 0), t2)
                t3 = B.op("dve", lambda h, s=scale: h.tensor_scalar(
                    acc1[0:1, 1:2], acc1[0:1, 0:1], s, 1e-5, Op.mult, Op.max), t2)
                t4 = B.op("dve", lambda h: h.tensor_scalar(
                    acc1[0:1, 2:3], acc1[0:1, 1:2], 0.5, None, Op.mult), t3)
                tm3 = B.op("pe", lambda h: h.matmul(pk[0][:, 0:2], lhsT=ones_f[0:1, 0:128],
                           rhs=acc1[0:1, 1:3], start=True, stop=True), t4, *rr(("pk", 0)))
                t5 = B.op("dve", lambda h, sc=scol: h.tensor_copy(
                    sca[:, sc:sc + 1], pk[0][:, 0:1]), tm3)
                t6 = B.op("dve", lambda h, hc=hcol: h.tensor_copy(
                    sca[:, hc:hc + 1], pk[0][:, 1:2]), tm3)
                t7 = B.op("dve", lambda h, hc=hcol: h.tensor_scalar(
                    sca[:, hc + 1:hc + 2], sca[:, hc:hc + 1], -1.0, None, Op.mult), t6)
                rset(("pk", 0), t6)
                sca_toks += [t5, t7]
            sca_hold[0] = sca_toks[-1]
            if debug:
                B.dma("act", d_dbg, lambda h: h.dma_start(dbg["d_sc"], sca[0:1, 0:6]),
                      *sca_toks)

        def tern_unit(i, src_ap, dst_ap, hcol):
            sl = i % 2
            stg = stage[:, sl, :]
            scr = y2_sc[:, 0, sl * 1024:(sl + 1) * 1024]
            out8 = st8[:, sl, 0:1024]
            dst3 = stg.rearrange("p (k c) -> p k c", c=512) if src_ap.ndim == 3 else stg
            td = B.dma("sp", d_st[sl], lambda h, dst3=dst3, src_ap=src_ap:
                       h.dma_start(dst3, src_ap), *rr(("stg", sl)))
            tc = B.op("gp", lambda h, scr=scr, stg=stg, hcol=hcol: h.tensor_scalar(
                scr, stg, sca[:, hcol:hcol + 1], sca[:, hcol + 1:hcol + 2],
                Op.min, Op.max), td, sca_hold[0],
                *(rr(("tscr", sl)) + rr("y2a")))
            ts = B.op("dve", lambda h, scr=scr, stg=stg: h.tensor_tensor(
                scr, stg, scr, Op.subtract), tc)
            rset(("stg", sl), ts)
            tn = B.op("act", lambda h, out8=out8, scr=scr: h.activation(
                out8, scr, AF.Sign), ts, *rr(("t8", sl)))
            rset(("tscr", sl), tn)
            radd("y2a", tn)
            tw = B.dma("act", d_t8[sl], lambda h, dst_ap=dst_ap, out8=out8:
                       h.dma_start(dst_ap, out8), tn)
            rset(("t8", sl), tw)
            return tw

        tern_jobs = []
        tern_gu = [None] * 32
        tern_do = [None] * 64
        ui = 0
        for j in range(32 if LVL >= 2 else 0):
            for q in range(8):
                src = wgu_h[j, q * 256:(q + 1) * 256, :].rearrange(
                    "(k p) c -> p k c", p=128)
                tern_jobs.append((ui, src, sgu_h[j, :, q * 1024:(q + 1) * 1024], 0,
                                  ("gu", j)))
                ui += 1
        for kc in range(64 if LVL >= 2 else 0):
            for q in range(2):
                src = wdo_h[kc * 128:(kc + 1) * 128, q * 1024:(q + 1) * 1024]
                tern_jobs.append((ui, src, sdo_h[kc, :, q * 1024:(q + 1) * 1024], 2,
                                  ("do", kc)))
                ui += 1

        def pump_tern(k):
            if sca_hold[0] is None:
                return
            while k > 0 and tern_jobs:
                a = tern_jobs.pop(0)
                tw = tern_unit(*a[:4])
                kind, idx = a[4]
                if kind == "gu":
                    tern_gu[idx] = (tern_gu[idx] or []) + [tw]
                else:
                    tern_do[idx] = (tern_do[idx] or []) + [tw]
                k -= 1

        def stats2048(src_ap, *dep):
            ts = None
            for j in range(4):
                ts = B.op("dve", lambda h, j=j, src_ap=src_ap: h.bn_stats(
                    bnb[:, j * 6:(j + 1) * 6], src_ap[:, j * 512:(j + 1) * 512]),
                    *(list(dep) + [ts] + rr("bnb")))
            ta = B.op("dve", lambda h: h.bn_aggr(bn2[:], bnb[:, 0:24]), ts, *rr("bn2"))
            rset("bnb", ta)
            rset("bn2", ta)
            return ta

        def recip_sqrt(col, src_col, eps, *dep):
            t0 = B.op("dve", lambda h, src_col=src_col, eps=eps: h.tensor_scalar(
                svec[:, 30:31], src_col, eps, None, Op.add),
                *(list(dep) + rr("sv30") + rr("bn2")))
            radd("bn2", t0)
            ta = B.op("act", lambda h: h.activation(
                svec[:, 30:31], svec[:, 30:31], AF.Sqrt), t0)
            rset("sv30", ta)
            tb = B.op("dve", lambda h, col=col: h.reciprocal(
                svec[:, col:col + 1], svec[:, 30:31]), ta, *rr(("sv", col)))
            radd("sv30", tb)
            rset(("sv", col), tb)
            return tb

        wblk_tok = [[], []]
        xs_tok = [[], []]
        kt_tok = {}
        v_tok = [None] * NT
        v_rd = []
        q_tok = [None] * NT
        qrd_tok = [[] for _ in range(NT)]
        r1_tok = [None] * NT
        x1_tok = [None] * NT
        po_tok = None
        arb_hi_war = [t_cos]
        out_toks = []

        for half in range(2 if LVL >= 3 else 0):
            for qq in range(2):
                qtiles = list(range(half * 8 + qq * 4, half * 8 + qq * 4 + 4))
                if (half, qq) != (0, 0):
                    fin_mean()
                P1N = int(os.environ.get("P1N", "9"))
                P1T = int(os.environ.get("P1T", "99"))
                for ti, t in enumerate(qtiles):
                    if t >= P1T:
                        continue
                    if (half, qq) == (0, 0):
                        pump_mean(12)
                    sl = t % 2
                    tx = B.dma("sp", d_x[sl], lambda h, sl=sl, t=t: h.dma_start(
                        xs[:, sl, :], x_h[t * 128:(t + 1) * 128, :]), *xs_tok[sl])
                    xs_tok[sl] = []
                    if P1N < 2:
                        continue
                    tst = stats2048(xs[:, sl, :], tx)
                    if P1N < 3:
                        continue
                    tr1 = recip_sqrt(2, bn2[:, 1:2], 1e-5, tst)
                    tcp = B.op("dve", lambda h, t=t: h.tensor_copy(
                        r1_all[:, t:t + 1], svec[:, 2:3]), tr1, *rr(("r1w", t)))
                    rset(("r1w", t), tcp)
                    r1_tok[t] = tcp
                    td = B.op("dve", lambda h: h.tensor_tensor(
                        svec[:, 3:4], bn2[:, 1:2], svec[:, 2:3], Op.mult), tr1, tst,
                        *rr(("sv", 3)))
                    radd("bn2", td)
                    td = B.op("dve", lambda h: h.tensor_tensor(
                        svec[:, 3:4], svec[:, 3:4], svec[:, 2:3], Op.mult), td)
                    rset(("sv", 3), td)
                    trq = recip_sqrt(4, svec[:, 3:4], 1e-6, td)
                    radd(("sv", 3), trq)
                    t5 = B.op("dve", lambda h: h.tensor_tensor(
                        svec[:, 5:6], svec[:, 2:3], svec[:, 4:5], Op.mult), trq, tr1,
                        *rr(("sv", 5)))
                    radd(("sv", 2), t5, td, tcp)
                    radd(("sv", 4), t5)
                    rset(("sv", 5), t5)
                    if P1N < 4:
                        continue
                    ta = B.op("dve", lambda h, sl=sl: h.scalar_tensor_tensor(
                        y2_sc[:, 0, :], xs[:, sl, :], bn2[:, 0:1], rms_w_bc[:],
                        Op.subtract, Op.mult), tst, t_rb, tx,
                        *(rr("y2a") + rr("y2b")))
                    radd("bn2", ta)
                    xs_tok[sl].append(ta)
                    if P1N < 5:
                        continue
                    rmsbf = qx1[:, t, :]
                    tb = B.op("dve", lambda h, rb=rmsbf: h.tensor_scalar(
                        rb, y2_sc[:, 0, :], svec[:, 5:6], None, Op.mult), ta, t5,
                        q_tok[t], *qrd_tok[t])
                    radd(("sv", 5), tb)
                    rset("y2a", tb)
                    rset("y2b", tb)
                    if debug and t == 0:
                        tdmp = B.dma("act", d_dbg, lambda h, rb=rmsbf: h.dma_start(
                            dbg["d_rms"], rb), tb)
                        qrd_tok[t].append(tdmp)
                    if P1N < 6:
                        continue
                    hlds = []
                    for kc in range(16):
                        hlds.append(tp_put(
                            rmsbf[:, kc * 128:(kc + 1) * 128],
                            rmsT[:, kc, ti * 128:(ti + 1) * 128], 0, 128, "act",
                            [tb], rr(("rmsT", kc, ti)) +
                            (arb_hi_war if (ti == 0 and kc == 0) else [])))
                    tp_flush()
                    for kc in range(16):
                        rset(("rmsT", kc, ti), hlds[kc][1])
                    qrd_tok[t].append(hlds[-1][1])
                for nb in range(NBI if LVL >= 4 else 0):
                    if (half, qq) == (0, 0):
                        pump_mean(12)
                    gsl = nb % 2
                    twb = B.dma("gp", d_wb[gsl], lambda h, nb=nb, gsl=gsl: h.dma_start(
                        wblk[:, gsl], wi_h[nb].rearrange("(k p) c -> p k c", p=128)),
                        *(wblk_tok[gsl] + (rr(("pb", 1)) if gsl == 1 else [])))
                    wblk_tok[gsl] = []
                    for mt in range(4):
                        t = qtiles[mt]
                        bank = mt
                        tlast = None
                        for kc in range(16):
                            tlast = B.op("pe", lambda h, kc=kc, mt=mt, bank=bank, gsl=gsl:
                                h.matmul(pk[bank][:, 0:256],
                                         lhsT=rmsT[:, kc, mt * 128:(mt + 1) * 128],
                                         rhs=wblk[:, gsl, kc, :],
                                         start=(kc == 0), stop=(kc == 15)),
                                twb, *(rr(("rmsT", kc, mt)) +
                                       (rr(("pk", bank)) if kc == 0 else [])))
                            radd(("rmsT", kc, mt), tlast)
                        wblk_tok[gsl].append(tlast)
                        if gsl == 1:
                            radd(("pb", 1), tlast)
                        pb = pk[bank][:, 0:256]
                        co = tabs[:, 1, t * 64:(t + 1) * 64]
                        si = tabs[:, 0, t * 64:(t + 1) * 64]
                        if nb < 10:
                            scl = 0.125 if nb < 8 else 1.0
                            t1 = t2a = t2b = None
                            ya = rr("y2a")
                            yb = rr("y2b")
                            for u in range(4):
                                o0 = u * 64
                                t1 = B.op("dve", lambda h, pb=pb, co=co, scl=scl, o0=o0:
                                    h.scalar_tensor_tensor(
                                        y2_sc[:, 0, o0:o0 + 64],
                                        pb[:, o0:o0 + 64], scl, co[:, 0:64],
                                        Op.mult, Op.mult), tlast,
                                    *(ya if u == 0 else []))
                                t2a = B.op("dve", lambda h, pb=pb, si=si, scl=scl, o0=o0:
                                    h.scalar_tensor_tensor(
                                        y2_sc[:, 1, o0:o0 + 32],
                                        pb[:, o0 + 32:o0 + 64], -scl, si[:, 0:32],
                                        Op.mult, Op.mult), tlast,
                                    *((yb + rr(("ss", 1))) if u == 0 else []))
                                t2b = B.op("dve", lambda h, pb=pb, si=si, scl=scl, o0=o0:
                                    h.scalar_tensor_tensor(
                                        y2_sc[:, 1, o0 + 32:o0 + 64],
                                        pb[:, o0:o0 + 32], scl, si[:, 0:32],
                                        Op.mult, Op.mult), tlast)
                            radd("tabs1", t1)
                            radd("tabs0", t2a, t2b)
                            rset(("pk", bank), t2b)
                            radd(("pk", bank), t1, t2a)
                        if nb < 8:
                            qv = qx1[:, t, nb * 256:(nb + 1) * 256]
                            tq = B.op("dve", lambda h, qv=qv: h.tensor_tensor(
                                qv, y2_sc[:, 0, 0:256], y2_sc[:, 1, 0:256], Op.add),
                                t1, t2a, t2b, *qrd_tok[t])
                            rset("y2a", tq)
                            rset("y2b", tq)
                            q_tok[t] = tq
                            if debug and t == 0 and nb == 7:
                                tdq = B.dma("act", d_dbg, lambda h, t=t: h.dma_start(
                                    dbg["d_q"], qx1[:, t, :]), tq)
                                qrd_tok[t].append(tdq)
                        elif nb < 10:
                            tk_ev = None
                            for hv in range(2):
                                tqk = B.op("dve", lambda h, hv=hv: h.tensor_tensor(
                                    opair[:, :], y2_sc[:, 0, hv * 128:(hv + 1) * 128],
                                    y2_sc[:, 1, hv * 128:(hv + 1) * 128], Op.add),
                                    t1, t2a, t2b, *rr("opair"))
                                gpair = (nb - 8) * 2 + hv
                                tt, tev = transpose_ev(
                                    opair[:, :], kt_v[:, gpair, t * 128:(t + 1) * 128],
                                    0, 128, "act", [tqk], rr(("kt", gpair, t)))
                                rset("opair", tt)
                                rset(("kt", gpair, t), tev)
                                kt_tok[(gpair, t)] = tev
                                tk_ev = tev
                            rset("y2a", tk_ev)
                            rset("y2b", tk_ev)
                        else:
                            voff = (nb - 10) * 256
                            tv = B.op("act", lambda h, bank=bank, t=t, voff=voff:
                                h.activation(v_v[:, t, voff:voff + 256],
                                             pk[bank][:, 0:256], AF.Copy),
                                tlast, v_tok[t])
                            v_tok[t] = tv
                            rset(("pk", bank), tv)
                            if debug and t == 0 and nb == 11:
                                tdv = B.dma("act", d_dbg, lambda h, t=t: h.dma_start(
                                    dbg["d_v"], v_v[:, t, :]), tv)
                                v_tok[t] = tdv
                if debug and qtiles[0] == 12:
                    kd = [kt_tok.get((0, t2)) for t2 in range(NT)]
                    tdk = B.dma("act", d_dbg, lambda h: h.dma_start(
                        dbg["d_kt"], kt_v[:, 0, :]), *kd)
                    for t2 in range(NT):
                        radd(("kt", 0, t2), tdk)
                rms_war = [wblk_tok[0][-1] if wblk_tok[0] else None,
                           wblk_tok[1][-1] if wblk_tok[1] else None]
                for kc in range(16):
                    for ti in range(4):
                        radd(("rmsT", kc, ti), *rms_war)

                for rr_i in range(2 if LVL >= 5 else 0):
                    rtiles = qtiles[rr_i * 2: rr_i * 2 + 2]
                    a1v = {0: ssat, 1: y2_sc[:, 1, :]}
                    ss_bufs = [ssat, y2_sc[:, 1, :]]
                    pb_bufs = [arb[:, 26624:28672], arb[:, 20480:22528]]
                    SVC = [(7, 8, 9), (26, 27, 28)]

                    def emit_scores(t, mt, hh, L, nch):
                        par = hh % 2
                        ssv = ss_bufs[hh % 2]
                        g_ = hh // 4
                        gpair = g_ // 2
                        base = (g_ % 2) * 64
                        col = (hh // 8) * 4 + (hh % 4)
                        lhs = qt_v[base:base + 64, col, :]
                        sspar = hh % 2
                        nm, dn, rd = SVC[par]
                        kdeps = [kt_tok.get((gpair, t2)) for t2 in range(t + 1)]
                        sc_toks = []
                        for c in range(nch):
                            c0, c1 = c * 512, min(L, (c + 1) * 512)
                            tmm = B.op("pe", lambda h, c0=c0, c1=c1, lhs=lhs,
                                       gpair=gpair, base=base, c=c: h.matmul(
                                pk[c][:, 0:c1 - c0], lhsT=lhs,
                                rhs=kt_v[base:base + 64, gpair, c0:c1],
                                start=True, stop=True),
                                *([kdeps[-1]] + kdeps + rr(("pk", c)) +
                                  rr(("qt", col, base))))
                            radd(("qt", col, base), tmm)
                            sc_toks.append(tmm)
                        ev_toks = []
                        for c in range(nch):
                            c0, c1 = c * 512, min(L, (c + 1) * 512)
                            re_ = c1 if c1 < L else L - 128
                            if re_ > c0:
                                te = B.op("dve", lambda h, c0=c0, re=re_, c=c,
                                          ssv=ssv: h.tensor_copy(
                                    ssv[:, c0:re], pk[c][:, 0:re - c0]),
                                    sc_toks[c], *rr(("ss", sspar)))
                                ev_toks.append(te)
                            if c1 == L:
                                te = B.op("dve", lambda h, c0=c0, c1=c1, c=c, L=L,
                                          ssv=ssv: h.tensor_tensor(
                                    ssv[:, L - 128:L],
                                    pk[c][:, L - 128 - c0:c1 - c0],
                                    mask[:], Op.add),
                                    sc_toks[c], t_mask, *rr(("ss", sspar)))
                                ev_toks.append(te)
                            rset(("pk", c), ev_toks[-1])
                        rset(("ss", sspar), *ev_toks)
                        tmx = B.op("dve", lambda h, L=L, ssv=ssv, nm=nm: h.tensor_reduce(
                            svec[:, nm:nm + 1], ssv[:, 0:L], axis=AX.X, op=Op.max,
                            negate=True), *(ev_toks + rr(("sv", nm))))
                        rset(("sv", nm), tmx)
                        radd(("ss", sspar), tmx)
                        texp = B.op("act", lambda h, L=L, ssv=ssv, par=par, nm=nm,
                                    dn=dn: h.activation(
                            pb_bufs[par][:, 0:L], ssv[:, 0:L], AF.Exp,
                            bias=svec[:, nm:nm + 1], accum_out=svec[:, dn:dn + 1]),
                            tmx, *(rr(("pb", par)) + rr(("sv", dn)) + ev_toks))
                        radd(("sv", nm), texp)
                        rset(("sv", dn), texp)
                        rset(("pb", par), texp)
                        radd(("ss", sspar), texp)
                        trd = B.op("dve", lambda h, dn=dn, rd=rd: h.reciprocal(
                            svec[:, rd:rd + 1], svec[:, dn:dn + 1]), texp,
                            *rr(("sv", rd)))
                        radd(("sv", dn), trd)
                        rset(("sv", rd), trd)
                        tsc = B.op("gp", lambda h, L=L, par=par, rd=rd: h.tensor_scalar(
                            pb_bufs[par][:, 0:L], pb_bufs[par][:, 0:L],
                            svec[:, rd:rd + 1], 0.0, Op.mult, Op.add),
                            texp, trd, *rr(("pb", par)))
                        radd(("sv", rd), tsc)
                        rset(("pb", par), tsc)
                        if debug and t == 1 and hh == 0:
                            tds = B.dma("act", d_dbg, lambda h, ssv=ssv: h.dma_start(
                                dbg["d_ss"], ssv[:, 0:512]), tmx, texp)
                            radd(("ss", sspar), tds)
                            tdp = B.dma("act", d_dbg, lambda h, par=par: h.dma_start(
                                dbg["d_probs"], pb_bufs[par][:, 0:512]), texp)
                            radd(("pb", par), tdp)
                        return dict(t=t, mt=mt, hh=hh, L=L, par=par, g_=g_,
                                    texp=tsc, trd=trd, rd=rd, col=col, base=base)

                    def emit_av(st):
                        nonlocal po_tok
                        par = st["par"]
                        pbv = pb_bufs[par]
                        t = st["t"]
                        mt = st["mt"]
                        hh = st["hh"]
                        obase = (hh % 2) * 64
                        hl = []
                        for kc in range(t + 1):
                            hl.append(tp_put(
                                pbv[:, kc * 128:(kc + 1) * 128],
                                pbv[:, kc * 128:(kc + 1) * 128], 0, 128,
                                "act",
                                [st["texp"]], rr(("pb", par))))
                            radd(("pb", par), hl[-1][0])
                        tp_flush()
                        mm_tok = None
                        for kc in range(t + 1):
                            g_ = st["g_"]
                            mm_tok = B.op("pe", lambda h, kc=kc, g_=g_, t=t, pbv=pbv,
                                          obase=obase: h.matmul(
                                po[obase:obase + 64, 0:128],
                                lhsT=v_v[:, kc, g_ * 64:(g_ + 1) * 64],
                                rhs=pbv[:, kc * 128:(kc + 1) * 128],
                                start=(kc == 0), stop=(kc == t)),
                                hl[kc][1], v_tok[kc], po_tok)
                            radd(("pb", par), mm_tok)
                            if kc == t:
                                v_rd.append(mm_tok)
                        po_tok = mm_tok
                        if hh % 2 == 1:
                            kc2 = hh // 2
                            tev = B.op("act", lambda h, mt=mt, kc2=kc2: h.activation(
                                ot_v[:, mt, kc2, :], po[:, 0:128], AF.Copy),
                                mm_tok, *rr(("ot", mt, kc2)))
                            po_tok = tev
                            rset(("ot", mt, kc2), tev)
                        return rr(("qt", st["col"], st["base"]))[-1]

                    for mt, t in enumerate(rtiles):
                        qhl = []
                        for hh in range(NH):
                            col = (hh // 8) * 4 + (hh % 4)
                            base = ((hh // 4) % 2) * 64
                            qhl.append((col, base, tp_put(
                                qx1[:, t, hh * 64:(hh + 1) * 64],
                                qt_v[base:base + 64, col, :], base, 64, "act",
                                [q_tok[t], x1_tok[t]] + (rms_war if hh == 0 else []),
                                rr(("qt", col, base)))))
                        tp_flush()
                        for col, base, hld in qhl:
                            rset(("qt", col, base), hld[1])
                        last_q_rd = None
                        L = (t + 1) * 128
                        nch = (L + 511) // 512
                        pend = []
                        for hh in range(NH):
                            stA = emit_scores(t, mt, hh, L, nch)
                            pend.append(stA)
                            if len(pend) > 1:
                                last_q_rd = emit_av(pend.pop(0))
                            if qq == 0 and half == 0:
                                pump_mean(3)
                            else:
                                pump_tern(1)
                        while pend:
                            last_q_rd = emit_av(pend.pop(0))
                        qrd_tok[t] = [last_q_rd]
                        if debug and t == 0:
                            od = [rr(("ot", 0, k2))[-1] for k2 in range(16)]
                            tdo = B.dma("act", d_dbg, lambda h: h.dma_start(
                                dbg["d_o"], ot_v[:, 0].rearrange("p a b -> p (a b)")),
                                *od)
                            for k2 in range(16):
                                radd(("ot", 0, k2), tdo)
                    xld = {}
                    for mt, t in enumerate(rtiles):
                        sl = t % 2
                        xld[t] = B.dma("sp", d_x[sl], lambda h, sl=sl, t=t: h.dma_start(
                            xs[:, sl, :], x_h[t * 128:(t + 1) * 128, :]), *xs_tok[sl])
                        xs_tok[sl] = []
                    for nb in range(NBO):
                        gsl = nb % 2
                        twb = B.dma("gp", d_wb[gsl], lambda h, nb=nb, gsl=gsl: h.dma_start(
                            wblk[:, gsl], wo_h[nb].rearrange("(k p) c -> p k c", p=128)),
                            *(wblk_tok[gsl] + (rr(("pb", 1)) if gsl == 1 else [])))
                        wblk_tok[gsl] = []
                        for mt, t in enumerate(rtiles):
                            bank = (nb % 2) * 2 + mt
                            tlast = None
                            for kc in range(16):
                                tlast = B.op("pe", lambda h, kc=kc, mt=mt, bank=bank,
                                             gsl=gsl: h.matmul(
                                    pk[bank][:, 0:256], lhsT=ot_v[:, mt, kc, :],
                                    rhs=wblk[:, gsl, kc, :],
                                    start=(kc == 0), stop=(kc == 15)),
                                    twb, *(rr(("ot", mt, kc)) +
                                           (rr(("pk", bank)) if kc == 0 else [])))
                                radd(("ot", mt, kc), tlast)
                            wblk_tok[gsl].append(tlast)
                            if gsl == 1:
                                radd(("pb", 1), tlast)
                            te = B.op("dve", lambda h, nb=nb, mt=mt, t=t, bank=bank:
                                h.scalar_tensor_tensor(
                                    a1v[mt][:, nb * 256:(nb + 1) * 256],
                                    xs[:, t % 2, nb * 256:(nb + 1) * 256],
                                    r1_all[:, t:t + 1], pk[bank][:, 0:256],
                                    Op.mult, Op.add),
                                tlast, xld[t], r1_tok[t],
                                *rr(("ss", mt)))
                            rset(("pk", bank), te)
                            radd(("ss", mt), te)
                            radd(("r1w", t), te)
                            xs_tok[t % 2].append(te)
                    for mt, t in enumerate(rtiles):
                        tst = stats2048(a1v[mt], *rr(("ss", mt)))
                        tr2 = recip_sqrt(11, bn2[:, 1:2], 1e-5, tst)
                        tt1 = B.op("dve", lambda h, mt=mt: h.tensor_scalar(
                            a1v[mt][:, :], a1v[mt][:, :], bn2[:, 0:1], svec[:, 11:12],
                            Op.subtract, Op.mult), tst, tr2)
                        radd("bn2", tt1)
                        radd(("sv", 11), tt1)
                        tx1 = B.op("dve", lambda h, mt=mt, t=t: h.tensor_tensor(
                            qx1[:, t, :], a1v[mt][:, :], xs[:, t % 2, :], Op.add),
                            tt1, xld[t], q_tok[t], *qrd_tok[t])
                        xs_tok[t % 2].append(tx1)
                        x1_tok[t] = tx1
                        rset(("ss", mt), tx1)
                        if debug and t == 0:
                            tdx = B.dma("act", d_dbg, lambda h, t=t: h.dma_start(
                                dbg["d_x1"], qx1[:, t, :]), tx1)
                            x1_tok[t] = tdx
                arb_hi_war = [wblk_tok[0][-1] if wblk_tok[0] else None,
                              wblk_tok[1][-1] if wblk_tok[1] else None, po_tok]

        fin_mean()
        pump_tern(10 ** 9)
        p3_done = ([wblk_tok[0][-1]] if wblk_tok[0] else []) + \
                  ([wblk_tok[1][-1]] if wblk_tok[1] else []) + \
                  [po_tok] + x1_tok + rr(("pb", 0)) + rr(("pb", 1)) + \
                  rr(("pb", 2)) + rr(("ss", 0)) + rr(("ss", 1)) + v_rd[-4:]
        sa_col = [18, 19]
        sy_col = [20, 21]
        aq_rd = [[], []]
        gq_war = arb_hi_war + rr(("pb", 0))
        for _sl in range(3):
            radd(("w3", _sl), *p3_done)
        radd(("w3", 2), *(rr(("t8", 0)) + rr(("t8", 1))))

        def quant_chain(src_ap, width, s_col, dst_bf, dep, war):
            toks = []
            for cc in range(width // 1024):
                sv = src_ap[:, cc * 1024:(cc + 1) * 1024]
                dv = dst_bf[:, cc * 1024:(cc + 1) * 1024]
                dsc = y2_sc[:, 0, 0:1024]
                csc = y2_sc[:, 0, 1024:2048]
                tq0 = B.op("act", lambda h, sv=sv, s_col=s_col: h.activation(
                    q32[:, :], sv, AF.Copy, scale=svec[:, s_col:s_col + 1]),
                    dep, *(rr("q32") + rr(("sv", s_col))))
                radd(("sv", s_col), tq0)
                td = B.op("dve", lambda h, sv=sv, s_col=s_col: h.scalar_tensor_tensor(
                    dsc, sv, svec[:, s_col:s_col + 1], q32[:, :],
                    Op.mult, Op.subtract), tq0, *rr("y2a"))
                radd(("sv", s_col), td)
                tc = B.op("gp", lambda h: h.tensor_scalar(
                    csc, dsc, 0.5, -0.5, Op.min, Op.max), td)
                tv2 = B.op("dve", lambda h: h.tensor_tensor(
                    dsc, dsc, csc, Op.subtract), tc)
                tsg = B.op("act", lambda h: h.activation(csc, dsc, AF.Sign), tv2, tq0)
                rset("y2a", tsg)
                ta = B.op("dve", lambda h, dv=dv: h.tensor_tensor(
                    dv, q32[:, :], csc, Op.add), tsg,
                    *(war if cc == 0 else []))
                radd("y2a", ta)
                rset("q32", ta)
                toks.append(ta)
            return toks

        down_last = None
        for gi in range(8 if LVL >= 6 else 0):
            gtiles = [2 * gi, 2 * gi + 1]
            aqT_done = [None, None]
            for mt, t in enumerate(gtiles):
                tst = stats2048(qx1[:, t, :], x1_tok[t])
                tr3 = recip_sqrt(12, bn2[:, 1:2], 1e-5, tst)
                tln = B.op("dve", lambda h, t=t: h.tensor_scalar(
                    ssat[:, :], qx1[:, t, :], bn2[:, 0:1], svec[:, 12:13],
                    Op.subtract, Op.mult), tst, tr3,
                    *(rr(("ss", 0)) + (p3_done if gi == 0 and mt == 0 else [])))
                radd("bn2", tln)
                radd(("sv", 12), tln)
                rset(("ss", 0), tln)
                tmx2 = B.op("dve", lambda h: h.tensor_reduce(
                    svec[:, 14:15], ssat[:, :], axis=AX.X, op=Op.max,
                    apply_absolute_value=True), tln, *rr(("sv", 14)))
                radd(("ss", 0), tmx2)
                tmx2 = B.op("dve", lambda h: h.tensor_scalar(
                    svec[:, 14:15], svec[:, 14:15], 1e-5, None, Op.max), tmx2)
                rset(("sv", 14), tmx2)
                trc = B.op("dve", lambda h: h.reciprocal(svec[:, 15:16], svec[:, 14:15]),
                           tmx2, *rr(("sv", 15)))
                radd(("sv", 14), trc)
                rset(("sv", 15), trc)
                ts1 = B.op("dve", lambda h: h.tensor_scalar(
                    svec[:, 16:17], svec[:, 15:16], 127.0, None, Op.mult), trc,
                    *rr(("sv", 16)))
                radd(("sv", 15), ts1)
                rset(("sv", 16), ts1)
                tis = B.op("dve", lambda h: h.reciprocal(svec[:, 17:18], svec[:, 16:17]),
                           ts1, *rr(("sv", 17)))
                radd(("sv", 16), tis)
                rset(("sv", 17), tis)
                tsa = B.op("dve", lambda h, c=sa_col[mt]: h.tensor_tensor(
                    svec[:, c:c + 1], svec[:, 17:18], sca[:, 4:5], Op.mult), tis,
                    sca_hold[0], *rr(("sv", sa_col[mt])))
                radd(("sv", 17), tsa)
                rset(("sv", sa_col[mt]), tsa)
                war = aq_rd[0] + aq_rd[1] + (p3_done if gi == 0 and mt == 0 else [])
                aq_rd[mt] = []
                aqs = quant_chain(ssat, D, 16, aq_st, tln, war)
                radd(("ss", 0), *aqs)
                if debug and gi == 0 and mt == 0:
                    tda = B.dma("act", d_dbg, lambda h: h.dma_start(dbg["d_aq"], aq_st[:]),
                          *aqs)
                    aq_rd[mt].append(tda)
                ahl = []
                for kc in range(16):
                    ahl.append(tp_put(
                        aq_st[:, kc * 128:(kc + 1) * 128], aqT_v[:, mt, kc, :],
                        0, 128, "act", [aqs[kc // 8]], rr(("aqT", mt, kc))))
                    aq_rd[mt].append(ahl[-1][0])
                tp_flush()
                for kc in range(16):
                    rset(("aqT", mt, kc), ahl[kc][1])
                aqT_done[mt] = ahl[-1][1]
            g_done = [None, None]
            last_mm = {}
            for j in range(16):
                for half_ in range(2):
                    blkid = 2 * j + half_
                    for qq4 in range(4):
                        sl = (blkid * 4 + qq4) % 3
                        tld = B.dma("sp", d_w3[sl], lambda h, blkid=blkid, qq4=qq4, sl=sl:
                            h.dma_start(w8slots[sl],
                                        sgu_h[blkid, :, qq4 * 2048:(qq4 + 1) * 2048]),
                            *((tern_gu[blkid] or []) + rr(("w3", sl))))
                        rset(("w3", sl), None)
                        for mt in range(2):
                            bank = half_ * 2 + mt
                            for k4 in range(4):
                                kc = qq4 * 4 + k4
                                tmm = B.op("pe", lambda h, mt=mt, kc=kc, k4=k4, sl=sl,
                                           bank=bank: h.matmul(
                                    pk[bank][:, 0:512], lhsT=aqT_v[:, mt, kc, :],
                                    rhs=w8slots[sl][:, k4 * 512:(k4 + 1) * 512],
                                    start=(kc == 0), stop=(kc == 15)),
                                    tld, *(rr(("aqT", mt, kc)) +
                                           ([aqT_done[mt]] if kc == 0 else []) +
                                           (rr(("pk", bank)) if kc == 0 else [])))
                                radd(("aqT", mt, kc), tmm)
                                radd(("w3", sl), tmm)
                                last_mm[bank] = tmm
                for mt in range(2):
                    gb, ub = mt, 2 + mt
                    tsl = B.op("act", lambda h, gb=gb, mt=mt: h.activation(
                        sil_sc[:, :], pk[gb][:, 0:512], AF.Silu,
                        scale=svec[:, sa_col[mt]:sa_col[mt] + 1]),
                        last_mm[gb],
                        *(rr("sil") + rr(("sv", sa_col[mt])) +
                          (rr(("stg", 0)) if (gi == 0 and j == 0) else [])))
                    radd(("sv", sa_col[mt]), tsl)
                    rset("sil", tsl)
                    tgc = B.op("dve", lambda h, ub=ub, mt=mt, j=j: h.scalar_tensor_tensor(
                        g_v[:, mt, j * 512:(j + 1) * 512], pk[ub][:, 0:512],
                        svec[:, sa_col[mt]:sa_col[mt] + 1], sil_sc[:, :],
                        Op.mult, Op.mult), tsl, g_done[mt], last_mm[ub],
                        *(p3_done if (gi == 0 and j == 0) else []))
                    radd(("sv", sa_col[mt]), tgc)
                    radd("sil", tgc)
                    rset(("pk", gb), tgc)
                    rset(("pk", ub), tgc)
                    g_done[mt] = tgc
            if debug and gi == 0:
                tdg = B.dma("act", d_dbg, lambda h: h.dma_start(dbg["d_g"], g_v[:, 0, :]),
                      g_done[0])
                g_done[0] = tdg
            for mt, t in enumerate(gtiles):
                tmg = B.op("dve", lambda h, mt=mt: h.tensor_reduce(
                    svec[:, 22:23], g_v[:, mt, :], axis=AX.X, op=Op.max,
                    apply_absolute_value=True), g_done[mt], *rr(("sv", 22)))
                tmg = B.op("dve", lambda h: h.tensor_scalar(
                    svec[:, 22:23], svec[:, 22:23], 1e-5, None, Op.max), tmg)
                rset(("sv", 22), tmg)
                trc = B.op("dve", lambda h: h.reciprocal(svec[:, 23:24], svec[:, 22:23]),
                           tmg, *rr(("sv", 23)))
                radd(("sv", 22), trc)
                rset(("sv", 23), trc)
                ts2 = B.op("dve", lambda h: h.tensor_scalar(
                    svec[:, 24:25], svec[:, 23:24], 127.0, None, Op.mult), trc,
                    *rr(("sv", 24)))
                radd(("sv", 23), ts2)
                rset(("sv", 24), ts2)
                tis2 = B.op("dve", lambda h: h.reciprocal(svec[:, 25:26], svec[:, 24:25]),
                            ts2, *rr(("sv", 25)))
                radd(("sv", 24), tis2)
                rset(("sv", 25), tis2)
                tsy = B.op("dve", lambda h, c=sy_col[mt]: h.tensor_tensor(
                    svec[:, c:c + 1], svec[:, 25:26], sca[:, 5:6], Op.mult), tis2,
                    sca_hold[0], *rr(("sv", sy_col[mt])))
                radd(("sv", 25), tsy)
                rset(("sv", sy_col[mt]), tsy)
                war_gq = gq_war if (gi == 0 and mt == 0) else \
                    ([down_last] if down_last else [])
                gqs = quant_chain(g_v[:, mt, :], FF, 24, g_v[:, mt, :], g_done[mt],
                                  war_gq)
                if debug and gi == 0 and mt == 0:
                    tdgq = B.dma("act", d_dbg, lambda h: h.dma_start(dbg["d_gq"], g_v[:, 0, :]),
                          *gqs)
                    gqs = gqs[:-1] + [tdgq]
                for kcb in range(8):
                    dhl = []
                    for k in range(8):
                        kc = kcb * 8 + k
                        dhl.append(tp_put(
                            g_v[:, mt, kc * 128:(kc + 1) * 128], gqt_v8[:, k, :],
                            0, 128, "act", [gqs[kc // 8]], rr(("gqt8", k))))
                    tp_flush()
                    for k in range(8):
                        kc = kcb * 8 + k
                        sl = (kcb * 8 + k) % 3
                        tld = B.dma("sp", d_w3[sl], lambda h, kc=kc, sl=sl: h.dma_start(
                            w8slots[sl], sdo_h[kc]),
                            *((tern_do[kc] or []) + rr(("w3", sl))))
                        rset(("w3", sl), None)
                        rset(("gqt8", k), None)
                        for n in range(4):
                            tmm = B.op("pe", lambda h, kc=kc, sl=sl, k=k, n=n: h.matmul(
                                pk[n][:, 0:512], lhsT=gqt_v8[:, k, :],
                                rhs=w8slots[sl][:, n * 512:(n + 1) * 512],
                                start=(kc == 0), stop=(kc == 63)),
                                dhl[k][1], tld,
                                *(rr(("pk", n)) if kc == 0 else []))
                            radd(("gqt8", k), tmm)
                            radd(("w3", sl), tmm)
                            down_last = tmm
                ev = None
                evs = []
                for n in range(4):
                    ev = B.op("dve", lambda h, n=n, mt=mt: h.tensor_scalar(
                        y2_sc[:, 1, n * 512:(n + 1) * 512], pk[n][:, 0:512],
                        svec[:, sy_col[mt]:sy_col[mt] + 1], None, Op.mult),
                        down_last, tsy, *(rr("y2b") if n == 0 else []))
                    rset(("pk", n), ev)
                    radd(("sv", sy_col[mt]), ev)
                    evs.append(ev)
                rset("y2b", *evs)
                if debug and gi == 0 and mt == 0:
                    tdy = B.dma("act", d_dbg, lambda h: h.dma_start(
                        dbg["d_y2"], y2_sc[:, 1, :]), *evs)
                    radd("y2b", tdy)
                tst = stats2048(y2_sc[:, 1, :], *evs)
                tr4 = recip_sqrt(13, bn2[:, 1:2], 1e-5, tst)
                tt2 = B.op("dve", lambda h: h.tensor_scalar(
                    y2_sc[:, 1, :], y2_sc[:, 1, :], bn2[:, 0:1], svec[:, 13:14],
                    Op.subtract, Op.mult), tst, tr4, *(list(evs) + rr("y2b")))
                radd("bn2", tt2)
                radd(("sv", 13), tt2)
                tout = B.op("dve", lambda h, t=t: h.tensor_tensor(
                    y2_sc[:, 0, :], y2_sc[:, 1, :], qx1[:, t, :], Op.add), tt2,
                    x1_tok[t], *rr("y2a"))
                rset("y2b", tout)
                rset("y2a", tout)
                tst2 = B.dma("sp", d_out, lambda h, t=t: h.dma_start(
                    out_h[t * 128:(t + 1) * 128, :], y2_sc[:, 0, :]), tout)
                radd("y2a", tst2)
                out_toks.append(tst2)

        B.raw("act", lambda h: h.wait_ge(d_out.h, d_out.n))
        if debug:
            B.raw("act", lambda h: h.wait_ge(d_dbg.h, d_dbg.n))
        for dsm in [d_x[0], d_x[1], d_wb[0], d_wb[1], d_st[0], d_st[1],
                    d_t8[0], d_t8[1], d_sg[0], d_sg[1], d_sd[0], d_sd[1], d_misc]:
            B.raw("act", lambda h, dsm=dsm: h.wait_ge(dsm.h, dsm.n))
        B.flush(block)
    return nc, dbg


def _prep(w_in, w_out, rms_w, w_gu, w_down):
    wi = np.ascontiguousarray(w_in.T.reshape(D, NBI, 256).transpose(1, 0, 2))
    wo = np.ascontiguousarray(w_out.T.reshape(D, NBO, 256).transpose(1, 0, 2))
    rw = np.ascontiguousarray(np.asarray(rms_w).reshape(1, D))
    gu = np.ascontiguousarray(
        w_gu.T.reshape(D, 2, 16, 512).transpose(2, 1, 0, 3).reshape(32, D, 512))
    wd = np.ascontiguousarray(w_down.T)
    return wi, wo, rw, gu, wd


def kernel(L0_attn_x, L0_attn_w_in, L0_attn_w_out, L0_attn_rms_w,
           L0_ffn_w_gate_up, L0_ffn_w_down):
    x = np.ascontiguousarray(np.asarray(L0_attn_x, dtype=np.float32))
    wi, wo, rw, gu, wd = _prep(
        np.asarray(L0_attn_w_in, np.float32), np.asarray(L0_attn_w_out, np.float32),
        np.asarray(L0_attn_rms_w, np.float32),
        np.asarray(L0_ffn_w_gate_up, np.float32),
        np.asarray(L0_ffn_w_down, np.float32))
    nc, _ = build_program(debug=False)
    in_maps = [{"x": np.ascontiguousarray(x[b]), "w_in_blk": wi, "w_out_blk": wo,
                "rms_w": rw, "w_gu_pairs": gu, "w_do_T": wd} for b in range(8)]
    res = run_bass_kernel_spmd(nc, in_maps, core_ids=list(range(8)))
    return np.stack([np.asarray(r["out"], np.float32) for r in res.results], axis=0)
